# version 27
# baseline (speedup 1.0000x reference)
import numpy as np
import ml_dtypes
import concourse.bass as bass
import concourse.mybir as mybir
from concourse.bass_utils import run_bass_kernel_spmd
from contextlib import ExitStack

F32 = mybir.dt.float32
BF16 = mybir.dt.bfloat16
U32 = mybir.dt.uint32
AF = mybir.ActivationFunctionType
ALU = mybir.AluOpType
AX = mybir.AxisListType
bfd = ml_dtypes.bfloat16

B, T, D, H, KD, VD, FF = 2, 8192, 1024, 8, 128, 128, 4096
NCORE, GRP = 8, 4
CH = 128
NCH = T // CH
TOK = T * B // NCORE
NOC = TOK // CH
NTT = 4
TTW = TOK // NTT
PCOLS = 772
EPS_LN = 1e-5
EPS_L2 = 1e-6


class Ctr:
    def __init__(self, nc, ctx, name):
        self.h = ctx.enter_context(nc.semaphore(name))
        self.v = 0

    def inc(self, instr, n=1):
        instr.then_inc(self.h, n)
        self.v += n
        return instr


def build_program():
    nc = bass.Bass("TRN2", target_bir_lowering=False, debug=False)

    x_full = nc.dram_tensor("x_full", [T, D], F32, kind="ExternalInput")
    x_own = nc.dram_tensor("x_own", [TOK, D], F32, kind="ExternalInput")
    wcat = nc.dram_tensor("wcat", [D, PCOLS], F32, kind="ExternalInput")
    ln1w = nc.dram_tensor("ln1w", [D, 1], F32, kind="ExternalInput")
    ln1b = nc.dram_tensor("ln1b", [D, 1], F32, kind="ExternalInput")
    wo = nc.dram_tensor("wo", [D, D], F32, kind="ExternalInput")
    w1 = nc.dram_tensor("w1", [D, FF], F32, kind="ExternalInput")
    b1c = nc.dram_tensor("b1c", [FF, 1], F32, kind="ExternalInput")
    w2 = nc.dram_tensor("w2", [FF, D], F32, kind="ExternalInput")
    b2c = nc.dram_tensor("b2c", [D, 1], F32, kind="ExternalInput")
    ln2w = nc.dram_tensor("ln2w", [D, 1], F32, kind="ExternalInput")
    ln2b = nc.dram_tensor("ln2b", [D, 1], F32, kind="ExternalInput")
    fw1 = nc.dram_tensor("fw1", [D, FF], F32, kind="ExternalInput")
    fs1 = nc.dram_tensor("fs1", [FF, 1], F32, kind="ExternalInput")
    fw2 = nc.dram_tensor("fw2", [D, FF], F32, kind="ExternalInput")
    fs2 = nc.dram_tensor("fs2", [FF, 1], F32, kind="ExternalInput")
    fh1 = nc.dram_tensor("fh1", [FF, FF], F32, kind="ExternalInput")
    fsh1 = nc.dram_tensor("fsh1", [FF, 1], F32, kind="ExternalInput")
    fh2 = nc.dram_tensor("fh2", [FF, FF], F32, kind="ExternalInput")
    fsh2 = nc.dram_tensor("fsh2", [FF, 1], F32, kind="ExternalInput")
    fw3 = nc.dram_tensor("fw3", [FF, D], F32, kind="ExternalInput")
    fs3 = nc.dram_tensor("fs3", [D, 1], F32, kind="ExternalInput")
    identb_i = nc.dram_tensor("identb_i", [128, 128], BF16, kind="ExternalInput")
    identf_i = nc.dram_tensor("identf_i", [128, 128], F32, kind="ExternalInput")
    triuf_i = nc.dram_tensor("triuf_i", [128, 128], F32, kind="ExternalInput")
    mnegS_i = nc.dram_tensor("mnegS_i", [128, 256], F32, kind="ExternalInput")
    mnegU_i = nc.dram_tensor("mnegU_i", [128, 256], F32, kind="ExternalInput")
    gidx_i = nc.dram_tensor("gidx_i", [128, 64], U32, kind="ExternalInput")

    out_d = nc.dram_tensor("out", [TOK, D], F32, kind="ExternalOutput")

    ob_d = nc.dram_tensor("ob_d", [T, 256], BF16)
    og_d = nc.dram_tensor("og_d", [GRP * T, 256], BF16)
    xof_d = nc.dram_tensor("xof_d", [8, 128, TOK], F32)

    ctx = ExitStack()
    ctx1 = ExitStack()

    def S_(name, shape, dt=F32, c=None):
        return (c if c is not None else ctx).enter_context(nc.sbuf_tensor(name, shape, dt))

    identb = S_("identb", [128, 128], BF16)
    identf = S_("identf", [128, 128], F32)
    triuf = S_("triuf", [128, 128], F32)
    mnegSs = S_("mnegSs", [128, 256], F32)
    mnegUs = S_("mnegUs", [128, 256], F32)
    ones_row_b = S_("ones_row_b", [1, 128], BF16)
    ones_row_f = S_("ones_row_f", [1, 128], F32)
    ones_col_f = S_("ones_col_f", [128, 1], F32)
    epsc = S_("epsc", [128, 1], F32)
    epsl = S_("epsl", [128, 1], F32)
    neghalf = S_("neghalf", [128, 1], F32)
    eps1p = S_("eps1p", [1, 1], F32)
    nh1p = S_("nh1p", [1, 1], F32)
    gidx_s = S_("gidx_s", [128, 64], U32)

    ln2w_sb = S_("ln2w_sb", [128, 8], F32)
    ln2b_sb = S_("ln2b_sb", [128, 8], F32)
    bvec1 = S_("bvec1", [128, 32], F32)
    bvec2 = S_("bvec2", [128, 8], F32)
    svec1 = S_("svec1", [128, 32], F32)
    svec2 = S_("svec2", [128, 32], F32)
    svh1 = S_("svh1", [128, 32], F32)
    svh2 = S_("svh2", [128, 32], F32)
    sv3 = S_("sv3", [128, 8], F32)
    pbT = ctx.enter_context(nc.psum_tensor("pbT", [128, 1024], BF16))
    pb1 = ctx.enter_context(nc.psum_tensor("pb1", [128, 512], F32))
    pb2 = ctx.enter_context(nc.psum_tensor("pb2", [128, 512], F32))
    pb3 = ctx.enter_context(nc.psum_tensor("pb3", [128, 512], F32))
    pb4 = ctx.enter_context(nc.psum_tensor("pb4", [128, 512], F32))
    pb5 = ctx.enter_context(nc.psum_tensor("pb5", [128, 512], F32))
    pb6 = ctx.enter_context(nc.psum_tensor("pb6", [128, 512], F32))
    pb7 = ctx.enter_context(nc.psum_tensor("pb7", [128, 512], F32))

    sD = Ctr(nc, ctx, "sD")
    sV = Ctr(nc, ctx, "sV")
    sA = Ctr(nc, ctx, "sA")
    sT = Ctr(nc, ctx, "sT")
    sG = Ctr(nc, ctx, "sG")
    sO = Ctr(nc, ctx, "sO")

    t = nc.tensor; v = nc.vector; s = nc.scalar; g = nc.gpsimd; sy = nc.sync

    def TW():
        t.wait_ge(sV.h, sV.v); t.wait_ge(sA.h, sA.v); t.wait_ge(sD.h, sD.v); t.wait_ge(sG.h, sG.v)

    def VW():
        v.wait_ge(sT.h, sT.v); v.wait_ge(sA.h, sA.v); v.wait_ge(sD.h, sD.v); v.wait_ge(sG.h, sG.v)

    def AW():
        s.wait_ge(sT.h, sT.v); s.wait_ge(sV.h, sV.v); s.wait_ge(sD.h, sD.v); s.wait_ge(sG.h, sG.v)

    def GW():
        g.wait_ge(sT.h, sT.v); g.wait_ge(sV.h, sV.v); g.wait_ge(sA.h, sA.v); g.wait_ge(sD.h, sD.v)

    def SYW():
        sy.wait_ge(sT.h, sT.v); sy.wait_ge(sV.h, sV.v); sy.wait_ge(sA.h, sA.v)

    def vq(ins):
        sV.inc(ins); v.wait_ge(sV.h, sV.v)

    vqb = vq

    def aq(ins):
        sA.inc(ins); s.wait_ge(sA.h, sA.v)

    aqb = aq

    def tq(ins):
        sT.inc(ins)

    def ldvec(dst, src):
        sD.inc(sy.dma_start(out=dst[:], in_=src.ap().rearrange("(a p) o -> p (a o)", p=128),
                            allow_slow_non_contiguous=True), 16)

    with nc.Block() as block:
        def emit(_):
            sD.inc(sy.dma_start(out=identb[:], in_=identb_i[:, :]), 16)
            sD.inc(sy.dma_start(out=identf[:], in_=identf_i[:, :]), 16)
            sD.inc(sy.dma_start(out=triuf[:], in_=triuf_i[:, :]), 16)
            sD.inc(sy.dma_start(out=mnegSs[:], in_=mnegS_i[:, :]), 16)
            sD.inc(sy.dma_start(out=mnegUs[:], in_=mnegU_i[:, :]), 16)
            sD.inc(sy.dma_start(out=gidx_s[:], in_=gidx_i[:, :]), 16)
            ldvec(ln2w_sb, ln2w)
            ldvec(ln2b_sb, ln2b)
            ldvec(bvec1, b1c)
            ldvec(bvec2, b2c)
            ldvec(svec1, fs1)
            ldvec(svec2, fs2)
            ldvec(svh1, fsh1)
            ldvec(svh2, fsh2)
            ldvec(sv3, fs3)
            v.wait_ge(sD.h, sD.v)
            vq(v.memset(ones_row_b[:], 1.0))
            vq(v.memset(ones_row_f[:], 1.0))
            vq(v.memset(ones_col_f[:], 1.0))
            vq(v.memset(epsc[:], EPS_L2))
            vq(v.memset(epsl[:], EPS_LN))
            vq(v.memset(neghalf[:], -0.5))
            vq(v.memset(eps1p[:], EPS_LN))
            vq(v.memset(nh1p[:], -0.5))

            wt_f = S_("wt_f", [128, 2, PCOLS], F32, ctx1)
            wt_b = S_("wt_b", [128, 8, PCOLS], BF16, ctx1)
            lnw_sb = S_("lnw_sb", [128, 8], F32, ctx1)
            lnb_sb = S_("lnb_sb", [128, 8], BF16, ctx1)
            bias_row = S_("bias_row", [1, PCOLS], BF16, ctx1)
            xc = S_("xc", [128, 4, D], F32, ctx1)
            sqs = S_("sqs", [128, D], F32, ctx1)
            msum = S_("msum", [128, NCH], F32, ctx1)
            ssum = S_("ssum", [128, NCH], F32, ctx1)
            rstd_a = S_("rstd_a", [128, NCH], F32, ctx1)
            zt = S_("zt", [128, 2, D], BF16, ctx1)
            zf = S_("zf", [128, 2, 8, 128], BF16, ctx1)
            qt_s = S_("qt_s", [128, NCH, 256], BF16, ctx1)
            kt_s = S_("kt_s", [128, NCH, 256], BF16, ctx1)
            vt_s = S_("vt_s", [128, NCH, 256], BF16, ctx1)
            ab_s = S_("ab_s", [128, NCH, 4], F32, ctx1)
            l2s = S_("l2s", [128, NCH, 4], F32, ctx1)
            o_s = S_("o_s", [128, NCH, 256], BF16, ctx1)
            loga = S_("loga", [128, 2], F32, ctx1)
            gcol = S_("gcol", [128, 2], F32, ctx1)
            ngcol = S_("ngcol", [128, 2], F32, ctx1)
            gsum_row = S_("gsum_row", [1, 2], F32, ctx1)
            gamc = S_("gamc", [128, 2], F32, ctx1)
            gCc = S_("gCc", [128, 2], F32, ctx1)
            gCraw = S_("gCraw", [128, 2], F32, ctx1)
            grevc = S_("grevc", [128, 2], F32, ctx1)
            nbetg = S_("nbetg", [128, 2], F32, ctx1)
            growA = S_("growA", [1, 128], F32, ctx1)
            growB = S_("growB", [1, 128], F32, ctx1)
            ngrowA = S_("ngrowA", [1, 128], F32, ctx1)
            ngrowB = S_("ngrowB", [1, 128], F32, ctx1)
            qkf = S_("qkf", [128, 2, 512], BF16, ctx1)
            growx = S_("growx", [2, 1, 256], F32, ctx1)
            growx0 = S_("growx0", [1, 256], F32, ctx1)
            growx1 = S_("growx1", [1, 256], F32, ctx1)
            ngrowx0 = S_("ngrowx0", [1, 256], F32, ctx1)
            ngrowx1 = S_("ngrowx1", [1, 256], F32, ctx1)
            gsum_row1 = S_("gsum_row1", [1, 2], F32, ctx1)
            gcol_p = S_("gcol_p", [128, 2, 2], F32, ctx1)
            gCraw_p = S_("gCraw_p", [128, 2, 2], F32, ctx1)
            grevd_p = S_("grevd_p", [128, 2, 2], F32, ctx1)
            gamc_p = S_("gamc_p", [128, 2, 2], F32, ctx1)
            gCc_p = S_("gCc_p", [128, 2, 2], F32, ctx1)
            grevc_p = S_("grevc_p", [128, 2, 2], F32, ctx1)
            nbetg_p = S_("nbetg_p", [128, 2, 2], F32, ctx1)
            dmA_p = S_("dmA_p", [128, 2, 256], F32, ctx1)
            dmT_p = S_("dmT_p", [128, 2, 256], F32, ctx1)
            a2_p = S_("a2_p", [128, 2, 256], BF16, ctx1)
            at2_p = S_("at2_p", [128, 2, 256], BF16, ctx1)
            p2t2_p = S_("p2t2_p", [128, 2, 256], BF16, ctx1)
            xs2_p = S_("xs2_p", [128, 2, 2, 256], BF16, ctx1)
            ys2_p = S_("ys2_p", [128, 2, 2, 256], BF16, ctx1)
            r2 = S_("r2", [128, 2, 256], BF16, ctx1)
            vb2 = S_("vb2", [128, 256], BF16, ctx1)
            osc2 = S_("osc2", [128, 256], BF16, ctx1)
            st_b = S_("st_b", [128, 2, 128], BF16, ctx1)

            ldvec(lnw_sb, ln1w)
            sG.inc(g.dma_start(out=lnb_sb[:], in_=ln1b.ap().rearrange("(a p) o -> p (a o)", p=128),
                               allow_slow_non_contiguous=True), 16)
            wc_t = wcat.ap().rearrange("(a p) c -> a p c", p=128)
            vq(v.memset(st_b[:, 0], 0.0))
            vq(v.memset(st_b[:, 1], 0.0))
            for a in range(8):
                par = a % 2
                SYW()
                sD.inc(sy.dma_start(out=wt_f[:, par], in_=wc_t[a]), 16)
                VW()
                vq(v.tensor_copy(wt_b[:, a], wt_f[:, par]))
                TW()
                t.matmul(pb1[0:1, 0:512], lnb_sb[:, a:a+1], wt_b[:, a, 0:512],
                         start=(a == 0), stop=(a == 7))
                tq(t.matmul(pb2[0:1, 0:260], lnb_sb[:, a:a+1], wt_b[:, a, 512:772],
                            start=(a == 0), stop=(a == 7)))
                VW()
                vq(v.tensor_scalar_mul(wt_b[:, a], wt_f[:, par], lnw_sb[:, a:a+1]))
            AW()
            aq(s.copy(bias_row[:, 0:512], pb1[0:1, 0:512]))
            aq(s.copy(bias_row[:, 512:772], pb2[0:1, 0:260]))

            x_t = x_full.ap().rearrange("(n p) d -> n p d", p=128)

            a1_ev = {}
            for c in range(NCH):
                par = c % 4
                if c == 0:
                    VW()
                if c >= 4:
                    sy.wait_ge(sV.h, a1_ev[c - 4])
                sD.inc(sy.dma_start(out=xc[:, par], in_=x_t[c]), 16)
                v.wait_ge(sD.h, sD.v)
                vq(v.tensor_tensor(sqs[:], xc[:, par], xc[:, par], ALU.mult))
                vqb(v.tensor_reduce(msum[:, c:c+1], xc[:, par], AX.X, ALU.add))
                vqb(v.tensor_reduce(ssum[:, c:c+1], sqs[:], AX.X, ALU.add))
                a1_ev[c] = sV.v
            VW()
            vq(v.tensor_scalar_mul(msum[:], msum[:], 1.0 / D))
            vq(v.tensor_scalar_mul(ssum[:], ssum[:], 1.0 / D))
            vq(v.tensor_tensor(rstd_a[:], msum[:], msum[:], ALU.mult))
            vq(v.tensor_tensor(rstd_a[:], ssum[:], rstd_a[:], ALU.subtract))
            vq(v.tensor_scalar_add(rstd_a[:], rstd_a[:], epsl[:]))
            AW()
            aqb(s.activation(rstd_a[:], rstd_a[:], AF.Ln))
            aqb(s.activation(rstd_a[:], rstd_a[:], AF.Exp, scale=neghalf[:]))

            a2_end = {}
            a2_zev = {}
            for c in range(NCH):
                par = c % 2
                pqk = (pb1, pb2)[par]
                pva = (pb3, pb4)[par]
                tp = 512 * par
                par4 = c % 4
                if c >= 2:
                    gT, gV, gA = a2_end[c - 2]
                    t.wait_ge(sA.h, gA)
                    v.wait_ge(sT.h, gT)
                    s.wait_ge(sV.h, gV)
                if c >= 4:
                    sy.wait_ge(sV.h, a2_zev[c - 4])
                sD.inc(sy.dma_start(out=xc[:, par4], in_=x_t[c]), 16)
                v.wait_ge(sD.h, sD.v)
                sV.inc(v.tensor_scalar(zt[:, par], xc[:, par4], msum[:, c:c+1],
                                       rstd_a[:, c:c+1], ALU.subtract, ALU.mult))
                a2_zev[c] = sV.v
                ev_z = sV.v
                t.wait_ge(sV.h, ev_z)
                for half in range(2):
                    for a in range(4):
                        aa = half * 4 + a
                        ins = t.transpose(pbT[:, tp + a*128:tp + (a+1)*128],
                                          zt[:, par, aa*128:(aa+1)*128], identb[:])
                    sT.inc(ins)
                    ev_tr = sT.v
                    s.wait_ge(sT.h, ev_tr)
                    sA.inc(s.copy(zf[:, par, half*4:half*4+4], pbT[:, tp:tp+512]))
                    if half == 0:
                        t.wait_ge(sA.h, sA.v)
                ev_zf = sA.v
                t.wait_ge(sA.h, ev_zf)
                for a in range(8):
                    t.matmul(pqk[:, 0:512], zf[:, par, a], wt_b[:, a, 0:512],
                             start=(a == 0), stop=False)
                t.matmul(pqk[:, 0:512], ones_row_b[:], bias_row[:, 0:512],
                         start=False, stop=True)
                for a in range(8):
                    t.matmul(pva[:, 0:260], zf[:, par, a], wt_b[:, a, 512:772],
                             start=(a == 0), stop=False)
                sT.inc(t.matmul(pva[:, 0:260], ones_row_b[:], bias_row[:, 512:772],
                                start=False, stop=True))
                ev_pj = sT.v
                s.wait_ge(sT.h, ev_pj)
                aq(s.activation(qt_s[:, c], pqk[:, 0:256], AF.Silu))
                aq(s.activation(kt_s[:, c], pqk[:, 256:512], AF.Silu))
                aq(s.activation(vt_s[:, c], pva[:, 0:256], AF.Silu))
                sA.inc(s.activation(ab_s[:, c], pva[:, 256:260], AF.Tanh, scale=neghalf[:]))
                ev_si = sA.v
                v.wait_ge(sA.h, ev_si)
                vq(v.tensor_scalar(ab_s[:, c], ab_s[:, c], -0.5, 0.5, ALU.mult, ALU.add))
                for i, (buf, lo) in enumerate(((qt_s, 0), (qt_s, 128), (kt_s, 0), (kt_s, 128))):
                    sl = buf[:, c, lo:lo+128]
                    vq(v.tensor_tensor(sqs[:, 0:128], sl, sl, ALU.mult))
                    vqb(v.tensor_reduce(l2s[:, c, i:i+1], sqs[:, 0:128], AX.X, ALU.add))
                a2_end[c] = (sT.v, sV.v, sA.v)

            l2v = l2s[:].rearrange("p a b -> p (a b)")
            AW()
            aqb(s.activation(l2v, l2v, AF.Ln, bias=epsc[:]))
            aqb(s.activation(l2v, l2v, AF.Exp, scale=neghalf[:]))
            VW()
            for c in range(NCH):
                for i, (buf, lo) in enumerate(((qt_s, 0), (qt_s, 128), (kt_s, 0), (kt_s, 128))):
                    sl = buf[:, c, lo:lo+128]
                    vq(v.tensor_scalar_mul(sl, sl, l2s[:, c, i:i+1]))

            endv = {}
            ev_st = None
            for c in range(NCH):
                p = c % 2
                gkq = (pb1, pb2)[p]
                dd = (pb3, pb4)[p]
                xy = (pb5, pb6)[p]
                tp = 512 * p
                grow_r = (growx0, growx1)[p]
                ngrow_r = (ngrowx0, ngrowx1)[p]
                gcol = gcol_p[:, p]
                gCraw = gCraw_p[:, p]
                grevd = grevd_p[:, p]
                gamc = gamc_p[:, p]
                gCc = gCc_p[:, p]
                grevc = grevc_p[:, p]
                nbetg = nbetg_p[:, p]
                dmA = dmA_p[:, p]
                dmT = dmT_p[:, p]
                a2_b = a2_p[:, p]
                at2_b = at2_p[:, p]
                p2t2_b = p2t2_p[:, p]
                qkp = qkf[:, p]
                bA = ab_s[:, c, 0:1]
                bB = ab_s[:, c, 1:2]

                if c >= 2:
                    gT, gV, gA = endv[c - 2]
                    t.wait_ge(sV.h, gV); t.wait_ge(sA.h, gA)
                    v.wait_ge(sT.h, gT); v.wait_ge(sA.h, gA)
                    s.wait_ge(sT.h, gT); s.wait_ge(sV.h, gV)

                sA.inc(s.activation(loga[:], ab_s[:, c, 2:4], AF.Ln))
                ev_a1 = sA.v
                t.wait_ge(sA.h, ev_a1)
                t.matmul(dd[:, 0:2], triuf[:], loga[:], start=True, stop=True)
                t.matmul(dd[0:1, 2:4], ones_col_f[:], loga[:], start=True, stop=True)
                t.matmul(dd[0:1, 4:132], loga[:, 0:1], triuf[:], start=True, stop=True)
                sT.inc(t.matmul(dd[0:1, 132:260], loga[:, 1:2], triuf[:], start=True, stop=True))
                ev_t1 = sT.v
                v.wait_ge(sT.h, ev_t1)
                vq(v.tensor_copy(gcol, dd[:, 0:2]))
                vq(v.tensor_copy(gsum_row1[:], dd[0:1, 2:4]))
                vq(v.tensor_copy(grow_r[:], dd[0:1, 4:260]))
                sV.inc(v.tensor_scalar_mul(ngrow_r[:], dd[0:1, 4:260], -1.0))
                ev_v1 = sV.v
                t.wait_ge(sV.h, ev_v1)
                sT.inc(t.matmul(dd[:, 260:262], ones_row_f[:], gsum_row1[:], start=True, stop=True))
                ev_t2 = sT.v
                v.wait_ge(sT.h, ev_t2)
                vq(v.tensor_copy(gCraw, dd[:, 260:262]))
                sV.inc(v.tensor_tensor(grevd, gCraw, gcol, ALU.subtract))
                ev_v2 = sV.v
                s.wait_ge(sV.h, ev_v2)
                aq(s.activation(gamc, gcol, AF.Exp))
                aq(s.activation(gCc, gCraw, AF.Exp))
                sA.inc(s.activation(grevc, grevd, AF.Exp))
                ev_a2 = sA.v
                v.wait_ge(sA.h, ev_a2)
                vq(v.scalar_tensor_tensor(nbetg, ab_s[:, c, 0:2], -1.0, gamc,
                                          ALU.mult, ALU.mult))
                v.wait_ge(sV.h, sV.v)
                ev_v3 = sV.v

                t.transpose(pbT[:, tp+0:tp+128], qt_s[:, c, 0:128], identb[:])
                t.transpose(pbT[:, tp+128:tp+256], kt_s[:, c, 0:128], identb[:])
                t.transpose(pbT[:, tp+256:tp+384], qt_s[:, c, 128:256], identb[:])
                sT.inc(t.transpose(pbT[:, tp+384:tp+512], kt_s[:, c, 128:256], identb[:]))
                ev_t3 = sT.v
                s.wait_ge(sT.h, ev_t3)
                sA.inc(s.copy(qkp, pbT[:, tp:tp+512]))
                ev_a3 = sA.v
                qfA, kfA = qkp[:, 0:128], qkp[:, 128:256]
                qfB, kfB = qkp[:, 256:384], qkp[:, 384:512]

                t.wait_ge(sA.h, ev_a3)
                t.matmul(gkq[:, 0:128], kfA, kfA, start=True, stop=True)
                t.matmul(gkq[:, 128:256], kfB, kfB, start=True, stop=True)
                t.matmul(gkq[:, 256:384], kfA, qfA, start=True, stop=True)
                t.matmul(gkq[:, 384:512], kfB, qfB, start=True, stop=True)
                for h in range(2):
                    o0 = h * 128
                    t.matmul(dd[:, o0:o0+128], ones_row_f[:], ngrow_r[0:1, o0:o0+128],
                             start=True, stop=False)
                    t.matmul(dd[:, o0:o0+128], grow_r[0:1, o0:o0+128], ones_row_f[:],
                             start=False, stop=True)
                    t.matmul(dd[:, 256+o0:256+o0+128], ones_row_f[:], grow_r[0:1, o0:o0+128],
                             start=True, stop=False)
                    t.matmul(dd[:, 256+o0:256+o0+128], ngrow_r[0:1, o0:o0+128], ones_row_f[:],
                             start=False, stop=True)
                if ev_st is not None:
                    t.wait_ge(sA.h, ev_st)
                t.matmul(pb7[:, 0:128], kfA, st_b[:, 0], start=True, stop=True)
                sT.inc(t.matmul(pb7[:, 128:256], kfB, st_b[:, 1], start=True, stop=True))
                ev_t4 = sT.v

                v.wait_ge(sT.h, ev_t4)
                vq(v.tensor_tensor(dmA, dd[:, 0:256], mnegSs[:], ALU.add))
                sV.inc(v.tensor_tensor(dmT, dd[:, 256:512], mnegUs[:], ALU.add))
                ev_v4 = sV.v
                s.wait_ge(sV.h, ev_v4)
                aq(s.activation(dmA, dmA, AF.Exp))
                sA.inc(s.activation(dmT, dmT, AF.Exp))
                ev_a4 = sA.v
                v.wait_ge(sA.h, ev_a4)
                vq(v.tensor_tensor(dmA, gkq[:, 0:256], dmA, ALU.mult))
                vq(v.tensor_scalar_mul(a2_b[:, 0:128], dmA[:, 0:128], bA))
                vq(v.tensor_scalar_mul(a2_b[:, 128:256], dmA[:, 128:256], bB))
                vq(v.tensor_tensor(p2t2_b, gkq[:, 256:512], dmT, ALU.mult))
                vq(v.tensor_scalar_mul(vb2[:, 0:128], vt_s[:, c, 0:128], bA))
                vq(v.tensor_scalar_mul(vb2[:, 128:256], vt_s[:, c, 128:256], bB))
                vq(v.scalar_tensor_tensor(r2[:, 0, 0:128], pb7[:, 0:128], nbetg[:, 0:1],
                                          vb2[:, 0:128], ALU.mult, ALU.add))
                sV.inc(v.scalar_tensor_tensor(r2[:, 0, 128:256], pb7[:, 128:256], nbetg[:, 1:2],
                                              vb2[:, 128:256], ALU.mult, ALU.add))
                v.wait_ge(sV.h, sV.v)
                ev_v5 = sV.v

                t.wait_ge(sV.h, ev_v5)
                t.transpose(pbT[:, tp+0:tp+128], a2_b[:, 0:128], identb[:])
                sT.inc(t.transpose(pbT[:, tp+128:tp+256], a2_b[:, 128:256], identb[:]))
                ev_t5 = sT.v
                s.wait_ge(sT.h, ev_t5)
                sA.inc(s.copy(at2_b, pbT[:, tp:tp+256]))
                ev_a5 = sA.v

                t.wait_ge(sA.h, ev_a5)
                t.matmul(gkq[:, 0:128], at2_b[:, 0:128], r2[:, 0, 0:128], start=True, stop=True)
                sT.inc(t.matmul(gkq[:, 128:256], at2_b[:, 128:256], r2[:, 0, 128:256],
                                start=True, stop=True))
                ev_t6 = sT.v
                v.wait_ge(sT.h, ev_t6)
                sV.inc(v.tensor_tensor(r2[:, 1], r2[:, 0], gkq[:, 0:256], ALU.subtract))
                ev_v6 = sV.v

                xcur, ycur = a2_b, at2_b
                rp = 1
                ev_a6 = None
                for lev in range(1, 7):
                    if ev_a6 is not None:
                        t.wait_ge(sA.h, ev_a6)
                    if lev < 6:
                        xn = xs2_p[:, p, lev % 2]
                        yn = ys2_p[:, p, lev % 2]
                        t.matmul(xy[:, 0:128], ycur[:, 0:128], xcur[:, 0:128], start=True, stop=True)
                        t.matmul(xy[:, 128:256], ycur[:, 128:256], xcur[:, 128:256], start=True, stop=True)
                        t.matmul(xy[:, 256:384], xcur[:, 0:128], ycur[:, 0:128], start=True, stop=True)
                        sT.inc(t.matmul(xy[:, 384:512], xcur[:, 128:256], ycur[:, 128:256],
                                        start=True, stop=True))
                        ev_t7 = sT.v
                        s.wait_ge(sT.h, ev_t7)
                        aq(s.copy(xn[:], xy[:, 0:256]))
                        sA.inc(s.copy(yn[:], xy[:, 256:512]))
                        ev_a6 = sA.v
                        xcur, ycur = xn, yn
                    else:
                        t.matmul(xy[:, 256:384], xcur[:, 0:128], ycur[:, 0:128], start=True, stop=True)
                        sT.inc(t.matmul(xy[:, 384:512], xcur[:, 128:256], ycur[:, 128:256],
                                        start=True, stop=True))
                        ev_t7 = sT.v
                        s.wait_ge(sT.h, ev_t7)
                        yn = ys2_p[:, p, 0]
                        sA.inc(s.copy(yn[:], xy[:, 256:512]))
                        ev_a6 = sA.v
                        ycur = yn
                    t.wait_ge(sA.h, ev_a6)
                    t.wait_ge(sV.h, ev_v6)
                    t.matmul(gkq[:, 0:128], ycur[:, 0:128], r2[:, rp, 0:128], start=True, stop=True)
                    sT.inc(t.matmul(gkq[:, 128:256], ycur[:, 128:256], r2[:, rp, 128:256],
                                    start=True, stop=True))
                    ev_t8 = sT.v
                    v.wait_ge(sT.h, ev_t8)
                    sV.inc(v.tensor_tensor(r2[:, 1 - rp], r2[:, rp], gkq[:, 0:256], ALU.add))
                    ev_v6 = sV.v
                    rp = 1 - rp

                t.wait_ge(sV.h, ev_v6)
                t.matmul(pb7[:, 0:128], qfA, st_b[:, 0], start=True, stop=True)
                t.matmul(pb7[:, 128:256], qfB, st_b[:, 1], start=True, stop=True)
                t.matmul(pb7[:, 256:384], p2t2_b[:, 0:128], r2[:, rp, 0:128], start=True, stop=True)
                sT.inc(t.matmul(pb7[:, 384:512], p2t2_b[:, 128:256], r2[:, rp, 128:256],
                                start=True, stop=True))
                ev_t9 = sT.v
                s.wait_ge(sT.h, ev_t9)
                aq(s.activation(osc2[:, 0:128], pb7[:, 0:128], AF.Identity, scale=gamc[:, 0:1]))
                sA.inc(s.activation(osc2[:, 128:256], pb7[:, 128:256], AF.Identity, scale=gamc[:, 1:2]))
                ev_a7 = sA.v
                v.wait_ge(sA.h, ev_a7)
                vq(v.tensor_tensor(o_s[:, c, 0:256], osc2[:], pb7[:, 256:512], ALU.add))
                vq(v.tensor_scalar_mul(vb2[:, 0:128], r2[:, rp, 0:128], grevc[:, 0:1]))
                vq(v.tensor_scalar_mul(vb2[:, 128:256], r2[:, rp, 128:256], grevc[:, 1:2]))
                vq(v.tensor_scalar_mul(osc2[:, 0:128], st_b[:, 0], gCc[:, 0:1]))
                sV.inc(v.tensor_scalar_mul(osc2[:, 128:256], st_b[:, 1], gCc[:, 1:2]))
                ev_v8 = sV.v
                t.wait_ge(sV.h, ev_v8)
                t.matmul(pb7[:, 0:256], identb[:], osc2[:], start=True, stop=False)
                t.matmul(pb7[:, 0:128], kt_s[:, c, 0:128], vb2[:, 0:128], start=False, stop=True)
                sT.inc(t.matmul(pb7[:, 128:256], kt_s[:, c, 128:256], vb2[:, 128:256],
                                start=False, stop=True))
                ev_t10 = sT.v
                s.wait_ge(sT.h, ev_t10)
                aq(s.copy(st_b[:, 0], pb7[:, 0:128]))
                sA.inc(s.copy(st_b[:, 1], pb7[:, 128:256]))
                ev_st = sA.v
                endv[c] = (sT.v, sV.v, sA.v)
                if c % 16 == 15:
                    q = c // 16
                    g.wait_ge(sV.h, sV.v)
                    sG.inc(g.dma_start(
                        out=ob_d.ap()[q*2048:(q+1)*2048, :].rearrange("(cc p) u -> p cc u", p=128),
                        in_=o_s[:, q*16:(q+1)*16]), 16)
                    g.wait_ge(sG.h, sG.v)
                    sG.inc(g.collective_compute(
                        "AllGather", ALU.bypass,
                        replica_groups=[[0, 1, 2, 3], [4, 5, 6, 7]],
                        ins=[ob_d.ap()[q*2048:(q+1)*2048, :].opt()],
                        outs=[og_d.ap()[q*8192:(q+1)*8192, :].opt()]), 1)

            g.wait_ge(sG.h, sG.v)

            SYW()
            sy.wait_ge(sG.h, sG.v)
            GW()
            ctx1.close()

            wo_b = S_("wo_b", [128, 8, 1024], BF16)
            stat_m = S_("stat_m", [1, TTW], F32)
            stat_q = S_("stat_q", [1, TTW], F32)
            stat_r = S_("stat_r", [1, TTW], F32)
            sq2 = S_("sq2", [128, TTW], F32)
            sqf = S_("sqf", [128, 4, TTW], F32)
            tsl = S_("tsl", [128, TTW], F32)
            lm = S_("lm", [128, 1], F32)
            mcr = S_("mcr", [128, 8], F32)
            lq = S_("lq", [128, 1], F32)
            lr = S_("lr", [128, 1], F32)
            x2t = S_("x2t", [128, D], F32)
            obuf = S_("obuf", [128, D], F32)
            wk_a = S_("wk_a", [128, 3, 32, 128], BF16)
            wk_b = S_("wk_b", [128, 3, 8, 128], BF16)
            ogt_r = S_("ogt_r", [128, 2, 1024], BF16)
            xstg = S_("xstg", [128, 1, D], F32)
            ogf_t = S_("ogf_t", [128, 8, TTW], BF16)
            x1t = S_("x1t", [128, 8, TTW], F32)
            n2t = S_("n2t", [128, 8, TTW], BF16)
            yt_f = S_("yt_f", [128, 8, TTW], F32)
            ybf = S_("ybf", [128, 8, TTW], BF16)
            hf = S_("hf", [128, 32, TTW], BF16)
            tf = S_("tf", [128, 32, TTW], BF16)

            sG.inc(g.dma_start(out=wo_b[:], in_=wo.ap().rearrange("(hc p) d -> p hc d", p=128)), 16)
            xo_t = x_own.ap().rearrange("(n p) d -> n p d", p=128)

            def ln_stats_rows(srcT):
                VW()
                for half in range(2):
                    for i4 in range(4):
                        dc = half * 4 + i4
                        ins = v.tensor_tensor(sqf[:, i4], srcT[:, dc], srcT[:, dc], ALU.mult)
                    sV.inc(ins)
                    t.wait_ge(sV.h, sV.v)
                    for i4 in range(4):
                        dc = half * 4 + i4
                        t.matmul(pb6[0:1, 0:TTW], ones_col_f[:], srcT[:, dc],
                                 start=(dc == 0), stop=(dc == 7))
                    for i4 in range(4):
                        dc = half * 4 + i4
                        ins2 = t.matmul(pb7[0:1, 0:TTW], ones_col_f[:], sqf[:, i4],
                                        start=(dc == 0), stop=(dc == 7))
                    tq(ins2)
                    if half == 0:
                        v.wait_ge(sT.h, sT.v)
                VW()
                vq(v.tensor_scalar_mul(stat_m[0:1, :], pb6[0:1, 0:TTW], 1.0 / D))
                vq(v.tensor_scalar_mul(stat_q[0:1, :], pb7[0:1, 0:TTW], 1.0 / D))
                vq(v.tensor_tensor(stat_r[0:1, :], stat_m[0:1, :], stat_m[0:1, :], ALU.mult))
                vq(v.tensor_tensor(stat_r[0:1, :], stat_q[0:1, :], stat_r[0:1, :], ALU.subtract))
                vq(v.tensor_scalar_add(stat_r[0:1, :], stat_r[0:1, :], eps1p[:]))
                AW()
                aqb(s.activation(stat_r[0:1, :], stat_r[0:1, :], AF.Ln))
                aqb(s.activation(stat_r[0:1, :], stat_r[0:1, :], AF.Exp, scale=nh1p[:]))
                TW()
                t.matmul(pb6[:, 0:TTW], ones_row_f[:], stat_m[0:1, :], start=True, stop=True)
                tq(t.matmul(pb7[:, 0:TTW], ones_row_f[:], stat_r[0:1, :], start=True, stop=True))

            def stream_w(slot, src_d, colblk, ntiles, ring_t):
                ap = src_d.ap().rearrange("(a p) o -> p a o", p=128)[:, :, colblk*128:(colblk+1)*128]
                sG.inc(g.dma_start(out=ring_t[:, slot, 0:ntiles], in_=ap), 16)

            def mm_stage(wd, ncb, nk, rhs_at, evac, wk=None, lhsT_at=None):
                banks = (pb3, pb4)
                gv = {}
                mm_ev = {}
                if wd is not None:
                    for pre in range(min(3, ncb)):
                        g.wait_ge(sT.h, sT.v)
                        stream_w(pre % 3, wd, pre, nk, wk)
                        gv[pre] = sG.v
                bank_gate = [None, None]
                for cb in range(ncb):
                    bank = banks[cb % 2]
                    if wd is not None:
                        t.wait_ge(sG.h, gv[cb])
                    if bank_gate[cb % 2] is not None:
                        sem, val = bank_gate[cb % 2]
                        t.wait_ge(sem, val)
                    for k in range(nk):
                        lh = wk[:, cb % 3, k] if wd is not None else lhsT_at(cb, k)
                        ins = t.matmul(bank[:, 0:TTW], lh, rhs_at(k),
                                       start=(k == 0), stop=(k == nk - 1))
                    tq(ins)
                    mmv = sT.v
                    mm_ev[cb] = mmv
                    if wd is not None and cb + 3 < ncb:
                        g.wait_ge(sT.h, mm_ev[cb])
                        stream_w(cb % 3, wd, cb + 3, nk, wk)
                        gv[cb + 3] = sG.v
                    bank_gate[cb % 2] = evac(cb, bank, mmv)

            for tt in range(NTT):
                for ci in range(TTW // 128):
                    cc = tt * (TTW // 128) + ci
                    par = 0
                    GW()
                    for i in range(4):
                        n = cc * 4 + i
                        sG.inc(g.indirect_dma_start(
                            ogt_r[:, par, i*256:(i+1)*256], None, og_d.ap(),
                            bass.IndirectOffsetOnAxis(ap=gidx_s[:, n:n+1], axis=0)), 16)
                    SYW()
                    sD.inc(sy.dma_start(out=xstg[:, par], in_=xo_t[cc]), 16)
                    TW()
                    for hc in range(8):
                        ins = t.transpose(pbT[:, hc*128:(hc+1)*128], ogt_r[:, par, hc*128:(hc+1)*128],
                                          identb[:])
                    tq(ins)
                    AW()
                    for hc in range(8):
                        ins = s.copy(ogf_t[:, hc, ci*128:(ci+1)*128], pbT[:, hc*128:(hc+1)*128])
                    aq(ins)
                    TW()
                    for a in range(8):
                        pbank = pb1 if a < 4 else pb2
                        ins = t.transpose(pbank[:, (a % 4)*128:(a % 4 + 1)*128],
                                          xstg[:, par, a*128:(a+1)*128], identf[:])
                    tq(ins)
                    AW()
                    for a in range(8):
                        pbank = pb1 if a < 4 else pb2
                        ins = s.copy(x1t[:, a, ci*128:(ci+1)*128], pbank[:, (a % 4)*128:(a % 4 + 1)*128])
                    aq(ins)

                GW()
                sG.inc(g.dma_start(out=xof_d.ap()[:, :, tt*TTW:(tt+1)*TTW].rearrange("a p n -> p a n"),
                                   in_=x1t[:]), 16)
                g.wait_ge(sG.h, sG.v)
                TW()
                def wo_evac(cb, bank, mmv):
                    v.wait_ge(sT.h, mmv)
                    vqb(v.scalar_tensor_tensor(x1t[:, cb], bank[:, 0:TTW], 1.0,
                                               x1t[:, cb], ALU.mult, ALU.add))
                    return (sV.h, sV.v)
                mm_stage(None, 8, 8, lambda k: ogf_t[:, k], wo_evac,
                         lhsT_at=lambda cb, k: wo_b[:, k, cb*128:(cb+1)*128])

                ln_stats_rows(x1t)
                for dc in range(8):
                    VW()
                    vq(v.tensor_tensor(sq2[:], x1t[:, dc], pb6[:, 0:TTW], ALU.subtract))
                    vq(v.tensor_tensor(sq2[:], sq2[:], pb7[:, 0:TTW], ALU.mult))
                    vq(v.tensor_scalar(n2t[:, dc], sq2[:], ln2w_sb[:, dc:dc+1],
                                       ln2b_sb[:, dc:dc+1], ALU.mult, ALU.add))

                TW()
                def w1_evac(cb, bank, mmv):
                    s.wait_ge(sT.h, mmv)
                    aq(s.activation(hf[:, cb], bank[:, 0:TTW], AF.Gelu, bias=bvec1[:, cb:cb+1]))
                    return (sA.h, sA.v)
                mm_stage(w1, 32, 8, lambda k: n2t[:, k], w1_evac, wk_a)
                TW()
                def w2_evac(cb, bank, mmv):
                    v.wait_ge(sT.h, mmv)
                    vqb(v.scalar_tensor_tensor(x1t[:, cb], bank[:, 0:TTW], bvec2[:, cb:cb+1],
                                               x1t[:, cb], ALU.add, ALU.add))
                    return (sV.h, sV.v)
                mm_stage(w2, 8, 32, lambda k: hf[:, k], w2_evac, wk_a)

                GW()
                sG.inc(g.dma_start(out=x1t[:], in_=xof_d.ap()[:, :, tt*TTW:(tt+1)*TTW].rearrange("a p n -> p a n"),
                                   accum_op=ALU.add), 16)
                g.wait_ge(sG.h, sG.v)
                SYW()
                sy.wait_ge(sG.h, sG.v)
                v.wait_ge(sG.h, sG.v)
                t.wait_ge(sG.h, sG.v)
                s.wait_ge(sG.h, sG.v)
                ln_stats_rows(x1t)
                for dc in range(8):
                    VW()
                    vq(v.tensor_tensor(yt_f[:, dc], x1t[:, dc], pb6[:, 0:TTW], ALU.subtract))
                    vq(v.tensor_tensor(yt_f[:, dc], yt_f[:, dc], pb7[:, 0:TTW], ALU.mult))
                    vq(v.tensor_copy(ybf[:, dc], yt_f[:, dc]))

                TW()
                w12_banks = ((pb3, pb4), (pb5, pb6))
                w12_gv = {}
                for pre in range(3):
                    g.wait_ge(sT.h, sT.v)
                    stream_w(pre % 3, fw1, pre, 8, wk_a)
                    stream_w(pre % 3, fw2, pre, 8, wk_b)
                    w12_gv[pre] = sG.v
                w12_gate = [None, None]
                tsl2 = (tsl, sq2)
                for cb in range(32):
                    bA, bB = w12_banks[cb % 2]
                    t.wait_ge(sG.h, w12_gv[cb])
                    if w12_gate[cb % 2] is not None:
                        sem, val = w12_gate[cb % 2]
                        t.wait_ge(sem, val)
                    for dc in range(8):
                        t.matmul(bA[:, 0:TTW], wk_a[:, cb % 3, dc], ybf[:, dc],
                                 start=(dc == 0), stop=(dc == 7))
                    for dc in range(8):
                        ins = t.matmul(bB[:, 0:TTW], wk_b[:, cb % 3, dc], ybf[:, dc],
                                       start=(dc == 0), stop=(dc == 7))
                    tq(ins)
                    mmv = sT.v
                    if cb + 3 < 32:
                        g.wait_ge(sT.h, mmv)
                        stream_w(cb % 3, fw1, cb + 3, 8, wk_a)
                        stream_w(cb % 3, fw2, cb + 3, 8, wk_b)
                        w12_gv[cb + 3] = sG.v
                    s.wait_ge(sT.h, mmv)
                    aq(s.activation(tsl2[cb % 2][:, 0:TTW], bA[:, 0:TTW], AF.Silu,
                                    scale=svec1[:, cb:cb+1]))
                    v.wait_ge(sA.h, sA.v)
                    vqb(v.scalar_tensor_tensor(tf[:, cb], bB[:, 0:TTW], svec2[:, cb:cb+1],
                                               tsl2[cb % 2][:, 0:TTW], ALU.mult, ALU.mult))
                    w12_gate[cb % 2] = (sV.h, sV.v)
                TW()
                def h1_evac(cb, bank, mmv):
                    s.wait_ge(sT.h, mmv)
                    aq(s.activation(hf[:, cb], bank[:, 0:TTW], AF.Gelu, scale=svh1[:, cb:cb+1]))
                    return (sA.h, sA.v)
                mm_stage(fh1, 32, 32, lambda k: tf[:, k], h1_evac, wk_a)
                TW()
                def h2_evac(cb, bank, mmv):
                    s.wait_ge(sT.h, mmv)
                    aq(s.activation(tf[:, cb], bank[:, 0:TTW], AF.Gelu, scale=svh2[:, cb:cb+1]))
                    return (sA.h, sA.v)
                mm_stage(fh2, 32, 32, lambda k: hf[:, k], h2_evac, wk_a)
                TW()
                def w3_evac(cb, bank, mmv):
                    s.wait_ge(sT.h, mmv)
                    aq(s.activation(x1t[:, cb], bank[:, 0:TTW], AF.Identity, scale=sv3[:, cb:cb+1]))
                    v.wait_ge(sA.h, sA.v)
                    vq(v.tensor_tensor(x1t[:, cb], x1t[:, cb], yt_f[:, cb], ALU.add))
                    return (sV.h, sV.v)
                mm_stage(fw3, 8, 32, lambda k: tf[:, k], w3_evac, wk_a)

                ln_stats_rows(x1t)
                TW()
                for ci in range(4):
                    t.transpose(pb5[:, ci:ci+1], stat_m[0:1, ci*128:(ci+1)*128], identf[0:1, 0:1])
                for ci in range(4):
                    ins = t.transpose(pb5[:, 4+ci:5+ci], stat_r[0:1, ci*128:(ci+1)*128], identf[0:1, 0:1])
                tq(ins)
                v.wait_ge(sT.h, sT.v)
                sV.inc(v.tensor_copy(mcr[:], pb5[:, 0:8]))
                v.wait_ge(sV.h, sV.v)
                ev_mcr = sV.v
                dma_ev = {}
                for ci in range(TTW // 128):
                    cc = tt * (TTW // 128) + ci
                    ob = (obuf, x2t)[ci % 2]
                    t.wait_ge(sV.h, sV.v)
                    for a in range(8):
                        pbank = pb1 if a < 4 else pb2
                        ins = t.transpose(pbank[:, (a % 4)*128:(a % 4 + 1)*128],
                                          x1t[:, a, ci*128:(ci+1)*128], identf[:])
                    tq(ins)
                    v.wait_ge(sT.h, sT.v)
                    if ci >= 2:
                        v.wait_ge(sO.h, dma_ev[ci - 2])
                    for a in range(8):
                        pbank = pb1 if a < 4 else pb2
                        ins = v.tensor_scalar(ob[:, a*128:(a+1)*128],
                                              pbank[:, (a % 4)*128:(a % 4 + 1)*128],
                                              mcr[:, ci:ci+1], mcr[:, 4+ci:5+ci],
                                              ALU.subtract, ALU.mult)
                        sV.inc(ins)
                    v.wait_ge(sV.h, sV.v)
                    sy.wait_ge(sV.h, sV.v)
                    sO.inc(sy.dma_start(out=out_d.ap().rearrange("(n p) d -> n p d", p=128)[cc],
                                        in_=ob[:]), 16)
                    dma_ev[ci] = sO.v

            sy.wait_ge(sO.h, sO.v)
            g.wait_ge(sO.h, sO.v)

        block.tensor(emit)
    ctx.close()
    return nc


_NC_CACHE = []


def kernel(**inputs):
    x = np.ascontiguousarray(np.asarray(inputs["L0_attn_x"], dtype=np.float32))
    Wq = np.asarray(inputs["L0_attn_Wq"], dtype=np.float32)
    Wk = np.asarray(inputs["L0_attn_Wk"], dtype=np.float32)
    Wv = np.asarray(inputs["L0_attn_Wv"], dtype=np.float32)
    Wb = np.asarray(inputs["L0_attn_Wb"], dtype=np.float32)
    Wa = np.asarray(inputs["L0_attn_Wa"], dtype=np.float32)
    Wo = np.ascontiguousarray(np.asarray(inputs["L0_attn_Wo"], dtype=np.float32))
    ln1w_np = np.ascontiguousarray(np.asarray(inputs["L0_attn_ln1_w"], np.float32).reshape(D, 1))
    ln1b_np = np.ascontiguousarray(np.asarray(inputs["L0_attn_ln1_b"], np.float32).reshape(D, 1))
    ln2w_np = np.ascontiguousarray(np.asarray(inputs["L0_attn_ln2_w"], np.float32).reshape(D, 1))
    ln2b_np = np.ascontiguousarray(np.asarray(inputs["L0_attn_ln2_b"], np.float32).reshape(D, 1))
    W1 = np.ascontiguousarray(np.asarray(inputs["L0_attn_W1"], np.float32))
    b1_np = np.ascontiguousarray(np.asarray(inputs["L0_attn_b1"], np.float32).reshape(FF, 1))
    W2 = np.ascontiguousarray(np.asarray(inputs["L0_attn_W2"], np.float32))
    b2_np = np.ascontiguousarray(np.asarray(inputs["L0_attn_b2"], np.float32).reshape(D, 1))
    fw1_np = np.ascontiguousarray(np.asarray(inputs["L0_ffn_w1_w"], np.float32).T)
    fs1_np = np.ascontiguousarray(np.asarray(inputs["L0_ffn_w1_s"], np.float32).reshape(FF, 1))
    fw2_np = np.ascontiguousarray(np.asarray(inputs["L0_ffn_w2_w"], np.float32).T)
    fs2_np = np.ascontiguousarray(np.asarray(inputs["L0_ffn_w2_s"], np.float32).reshape(FF, 1))
    fh1_np = np.ascontiguousarray(np.asarray(inputs["L0_ffn_h1_w"], np.float32).T)
    fsh1_np = np.ascontiguousarray(np.asarray(inputs["L0_ffn_h1_s"], np.float32).reshape(FF, 1))
    fh2_np = np.ascontiguousarray(np.asarray(inputs["L0_ffn_h2_w"], np.float32).T)
    fsh2_np = np.ascontiguousarray(np.asarray(inputs["L0_ffn_h2_s"], np.float32).reshape(FF, 1))
    fw3_np = np.ascontiguousarray(np.asarray(inputs["L0_ffn_w3_w"], np.float32).T)
    fs3_np = np.ascontiguousarray(np.asarray(inputs["L0_ffn_w3_s"], np.float32).reshape(D, 1))

    identb_np = np.eye(128, dtype=bfd)
    identf_np = np.eye(128, dtype=np.float32)
    triuf_np = np.triu(np.ones((128, 128), np.float32))
    mnegS_np = (-1e30 * (1 - np.tril(np.ones((128, 128), np.float32), -1))).astype(np.float32)
    mnegU_np = (-1e30 * (1 - triuf_np)).astype(np.float32)

    in_maps = []
    for c in range(NCORE):
        b, j = c // 4, c % 4
        gidx = np.empty((128, 64), np.uint32)
        for n in range(64):
            cc, i = n // 4, n % 4
            gidx[:, n] = j * 8192 + i * 2048 + cc * 128 + np.arange(128, dtype=np.uint32)
        in_maps.append(dict(
            x_full=np.ascontiguousarray(x[b]),
            x_own=np.ascontiguousarray(x[b, j*TOK:(j+1)*TOK]),
            wcat=np.ascontiguousarray(np.concatenate(
                [Wq[:, 256*j:256*j+256], Wk[:, 256*j:256*j+256], Wv[:, 256*j:256*j+256],
                 Wb[:, 2*j:2*j+2], Wa[:, 2*j:2*j+2]], axis=1)),
            ln1w=ln1w_np, ln1b=ln1b_np, wo=Wo,
            w1=W1, b1c=b1_np, w2=W2, b2c=b2_np, ln2w=ln2w_np, ln2b=ln2b_np,
            fw1=fw1_np, fs1=fs1_np, fw2=fw2_np, fs2=fs2_np,
            fh1=fh1_np, fsh1=fsh1_np, fh2=fh2_np, fsh2=fsh2_np,
            fw3=fw3_np, fs3=fs3_np,
            identb_i=identb_np, identf_i=identf_np, triuf_i=triuf_np,
            mnegS_i=np.concatenate([mnegS_np, mnegS_np], 1),
            mnegU_i=np.concatenate([mnegU_np, mnegU_np], 1), gidx_i=gidx,
        ))

    if not _NC_CACHE:
        _NC_CACHE.append(build_program())
    nc = _NC_CACHE[0]
    res = run_bass_kernel_spmd(nc, in_maps, core_ids=list(range(NCORE)))
    out = np.empty((B, T, D), np.float32)
    for c in range(NCORE):
        b, j = c // 4, c % 4
        out[b, j*TOK:(j+1)*TOK] = res.results[c]["out"]
    return out


# version 28
# speedup vs baseline: 1.0425x; 1.0425x over previous
import numpy as np
import ml_dtypes
import concourse.bass as bass
import concourse.mybir as mybir
from concourse.bass_utils import run_bass_kernel_spmd
from contextlib import ExitStack

F32 = mybir.dt.float32
BF16 = mybir.dt.bfloat16
U32 = mybir.dt.uint32
AF = mybir.ActivationFunctionType
ALU = mybir.AluOpType
AX = mybir.AxisListType
bfd = ml_dtypes.bfloat16

B, T, D, H, KD, VD, FF = 2, 8192, 1024, 8, 128, 128, 4096
NCORE, GRP = 8, 4
CH = 128
NCH = T // CH
TOK = T * B // NCORE
NOC = TOK // CH
NTT = 4
TTW = TOK // NTT
PCOLS = 772
EPS_LN = 1e-5
EPS_L2 = 1e-6


class Ctr:
    def __init__(self, nc, ctx, name):
        self.h = ctx.enter_context(nc.semaphore(name))
        self.v = 0

    def inc(self, instr, n=1):
        instr.then_inc(self.h, n)
        self.v += n
        return instr


def build_program():
    nc = bass.Bass("TRN2", target_bir_lowering=False, debug=False)

    x_full = nc.dram_tensor("x_full", [T, D], F32, kind="ExternalInput")
    x_own = nc.dram_tensor("x_own", [TOK, D], F32, kind="ExternalInput")
    wcat = nc.dram_tensor("wcat", [D, PCOLS], F32, kind="ExternalInput")
    ln1w = nc.dram_tensor("ln1w", [D, 1], F32, kind="ExternalInput")
    ln1b = nc.dram_tensor("ln1b", [D, 1], F32, kind="ExternalInput")
    wo = nc.dram_tensor("wo", [D, D], F32, kind="ExternalInput")
    w1 = nc.dram_tensor("w1", [D, FF], F32, kind="ExternalInput")
    b1c = nc.dram_tensor("b1c", [FF, 1], F32, kind="ExternalInput")
    w2 = nc.dram_tensor("w2", [FF, D], F32, kind="ExternalInput")
    b2c = nc.dram_tensor("b2c", [D, 1], F32, kind="ExternalInput")
    ln2w = nc.dram_tensor("ln2w", [D, 1], F32, kind="ExternalInput")
    ln2b = nc.dram_tensor("ln2b", [D, 1], F32, kind="ExternalInput")
    fw1 = nc.dram_tensor("fw1", [D, FF], F32, kind="ExternalInput")
    fs1 = nc.dram_tensor("fs1", [FF, 1], F32, kind="ExternalInput")
    fw2 = nc.dram_tensor("fw2", [D, FF], F32, kind="ExternalInput")
    fs2 = nc.dram_tensor("fs2", [FF, 1], F32, kind="ExternalInput")
    fh1 = nc.dram_tensor("fh1", [FF, FF], F32, kind="ExternalInput")
    fsh1 = nc.dram_tensor("fsh1", [FF, 1], F32, kind="ExternalInput")
    fh2 = nc.dram_tensor("fh2", [FF, FF], F32, kind="ExternalInput")
    fsh2 = nc.dram_tensor("fsh2", [FF, 1], F32, kind="ExternalInput")
    fw3 = nc.dram_tensor("fw3", [FF, D], F32, kind="ExternalInput")
    fs3 = nc.dram_tensor("fs3", [D, 1], F32, kind="ExternalInput")
    identb_i = nc.dram_tensor("identb_i", [128, 128], BF16, kind="ExternalInput")
    identf_i = nc.dram_tensor("identf_i", [128, 128], F32, kind="ExternalInput")
    triuf_i = nc.dram_tensor("triuf_i", [128, 128], F32, kind="ExternalInput")
    mnegS_i = nc.dram_tensor("mnegS_i", [128, 256], F32, kind="ExternalInput")
    mnegU_i = nc.dram_tensor("mnegU_i", [128, 256], F32, kind="ExternalInput")
    gidx_i = nc.dram_tensor("gidx_i", [128, 64], U32, kind="ExternalInput")

    out_d = nc.dram_tensor("out", [TOK, D], F32, kind="ExternalOutput")

    ob_d = nc.dram_tensor("ob_d", [T, 256], BF16)
    og_d = nc.dram_tensor("og_d", [GRP * T, 256], BF16)
    xof_d = nc.dram_tensor("xof_d", [8, 128, TOK], F32)

    ctx = ExitStack()
    ctx1 = ExitStack()

    def S_(name, shape, dt=F32, c=None):
        return (c if c is not None else ctx).enter_context(nc.sbuf_tensor(name, shape, dt))

    identb = S_("identb", [128, 128], BF16)
    identf = S_("identf", [128, 128], F32)
    triuf = S_("triuf", [128, 128], F32)
    mnegSs = S_("mnegSs", [128, 256], F32)
    mnegUs = S_("mnegUs", [128, 256], F32)
    ones_row_b = S_("ones_row_b", [1, 128], BF16)
    ones_row_f = S_("ones_row_f", [1, 128], F32)
    ones_col_f = S_("ones_col_f", [128, 1], F32)
    epsc = S_("epsc", [128, 1], F32)
    epsl = S_("epsl", [128, 1], F32)
    neghalf = S_("neghalf", [128, 1], F32)
    eps1p = S_("eps1p", [1, 1], F32)
    nh1p = S_("nh1p", [1, 1], F32)
    gidx_s = S_("gidx_s", [128, 64], U32)

    ln2w_sb = S_("ln2w_sb", [128, 8], F32)
    ln2b_sb = S_("ln2b_sb", [128, 8], F32)
    bvec1 = S_("bvec1", [128, 32], F32)
    bvec2 = S_("bvec2", [128, 8], F32)
    svec1 = S_("svec1", [128, 32], F32)
    svec2 = S_("svec2", [128, 32], F32)
    svh1 = S_("svh1", [128, 32], F32)
    svh2 = S_("svh2", [128, 32], F32)
    sv3 = S_("sv3", [128, 8], F32)
    pbT = ctx.enter_context(nc.psum_tensor("pbT", [128, 1024], BF16))
    pb1 = ctx.enter_context(nc.psum_tensor("pb1", [128, 512], F32))
    pb2 = ctx.enter_context(nc.psum_tensor("pb2", [128, 512], F32))
    pb3 = ctx.enter_context(nc.psum_tensor("pb3", [128, 512], F32))
    pb4 = ctx.enter_context(nc.psum_tensor("pb4", [128, 512], F32))
    pb5 = ctx.enter_context(nc.psum_tensor("pb5", [128, 512], F32))
    pb6 = ctx.enter_context(nc.psum_tensor("pb6", [128, 512], F32))
    pb7 = ctx.enter_context(nc.psum_tensor("pb7", [128, 512], F32))

    sD = Ctr(nc, ctx, "sD")
    sXC = [Ctr(nc, ctx, f"sXC{i}") for i in range(4)]
    sWA = [Ctr(nc, ctx, f"sWA{i}") for i in range(3)]
    sWB = [Ctr(nc, ctx, f"sWB{i}") for i in range(3)]
    sOG = [Ctr(nc, ctx, f"sOG{i}") for i in range(2)]
    sV = Ctr(nc, ctx, "sV")
    sA = Ctr(nc, ctx, "sA")
    sT = Ctr(nc, ctx, "sT")
    sG = Ctr(nc, ctx, "sG")
    sO = Ctr(nc, ctx, "sO")

    t = nc.tensor; v = nc.vector; s = nc.scalar; g = nc.gpsimd; sy = nc.sync

    def TW():
        t.wait_ge(sV.h, sV.v); t.wait_ge(sA.h, sA.v); t.wait_ge(sD.h, sD.v); t.wait_ge(sG.h, sG.v)

    def VW():
        v.wait_ge(sT.h, sT.v); v.wait_ge(sA.h, sA.v); v.wait_ge(sD.h, sD.v); v.wait_ge(sG.h, sG.v)

    def AW():
        s.wait_ge(sT.h, sT.v); s.wait_ge(sV.h, sV.v); s.wait_ge(sD.h, sD.v); s.wait_ge(sG.h, sG.v)

    def GW():
        g.wait_ge(sT.h, sT.v); g.wait_ge(sV.h, sV.v); g.wait_ge(sA.h, sA.v); g.wait_ge(sD.h, sD.v)

    def SYW():
        sy.wait_ge(sT.h, sT.v); sy.wait_ge(sV.h, sV.v); sy.wait_ge(sA.h, sA.v)

    def vq(ins):
        sV.inc(ins); v.wait_ge(sV.h, sV.v)

    vqb = vq

    def aq(ins):
        sA.inc(ins); s.wait_ge(sA.h, sA.v)

    aqb = aq

    def tq(ins):
        sT.inc(ins)

    def ldvec(dst, src):
        sD.inc(sy.dma_start(out=dst[:], in_=src.ap().rearrange("(a p) o -> p (a o)", p=128),
                            allow_slow_non_contiguous=True), 16)
        sy.wait_ge(sD.h, sD.v)

    with nc.Block() as block:
        def emit(_):
            for dst_, src_ in ((identb, identb_i), (identf, identf_i), (triuf, triuf_i),
                               (mnegSs, mnegS_i), (mnegUs, mnegU_i), (gidx_s, gidx_i)):
                sD.inc(sy.dma_start(out=dst_[:], in_=src_[:, :]), 16)
                sy.wait_ge(sD.h, sD.v)
            ldvec(ln2w_sb, ln2w)
            ldvec(ln2b_sb, ln2b)
            ldvec(bvec1, b1c)
            ldvec(bvec2, b2c)
            ldvec(svec1, fs1)
            ldvec(svec2, fs2)
            ldvec(svh1, fsh1)
            ldvec(svh2, fsh2)
            ldvec(sv3, fs3)
            v.wait_ge(sD.h, sD.v)
            vq(v.memset(ones_row_b[:], 1.0))
            vq(v.memset(ones_row_f[:], 1.0))
            vq(v.memset(ones_col_f[:], 1.0))
            vq(v.memset(epsc[:], EPS_L2))
            vq(v.memset(epsl[:], EPS_LN))
            vq(v.memset(neghalf[:], -0.5))
            vq(v.memset(eps1p[:], EPS_LN))
            vq(v.memset(nh1p[:], -0.5))

            wt_f = S_("wt_f", [128, 2, PCOLS], F32, ctx1)
            wt_b = S_("wt_b", [128, 8, PCOLS], BF16, ctx1)
            lnw_sb = S_("lnw_sb", [128, 8], F32, ctx1)
            lnb_sb = S_("lnb_sb", [128, 8], BF16, ctx1)
            bias_row = S_("bias_row", [1, PCOLS], BF16, ctx1)
            xc = S_("xc", [128, 4, D], F32, ctx1)
            sqs = S_("sqs", [128, D], F32, ctx1)
            msum = S_("msum", [128, NCH], F32, ctx1)
            ssum = S_("ssum", [128, NCH], F32, ctx1)
            rstd_a = S_("rstd_a", [128, NCH], F32, ctx1)
            zt = S_("zt", [128, 2, D], BF16, ctx1)
            zf = S_("zf", [128, 2, 8, 128], BF16, ctx1)
            qt_s = S_("qt_s", [128, NCH, 256], BF16, ctx1)
            kt_s = S_("kt_s", [128, NCH, 256], BF16, ctx1)
            vt_s = S_("vt_s", [128, NCH, 256], BF16, ctx1)
            ab_s = S_("ab_s", [128, NCH, 4], F32, ctx1)
            l2s = S_("l2s", [128, NCH, 4], F32, ctx1)
            o_s = S_("o_s", [128, NCH, 256], BF16, ctx1)
            loga = S_("loga", [128, 2], F32, ctx1)
            gcol = S_("gcol", [128, 2], F32, ctx1)
            ngcol = S_("ngcol", [128, 2], F32, ctx1)
            gsum_row = S_("gsum_row", [1, 2], F32, ctx1)
            gamc = S_("gamc", [128, 2], F32, ctx1)
            gCc = S_("gCc", [128, 2], F32, ctx1)
            gCraw = S_("gCraw", [128, 2], F32, ctx1)
            grevc = S_("grevc", [128, 2], F32, ctx1)
            nbetg = S_("nbetg", [128, 2], F32, ctx1)
            growA = S_("growA", [1, 128], F32, ctx1)
            growB = S_("growB", [1, 128], F32, ctx1)
            ngrowA = S_("ngrowA", [1, 128], F32, ctx1)
            ngrowB = S_("ngrowB", [1, 128], F32, ctx1)
            qkf = S_("qkf", [128, 2, 512], BF16, ctx1)
            growx = S_("growx", [2, 1, 256], F32, ctx1)
            growx0 = S_("growx0", [1, 256], F32, ctx1)
            growx1 = S_("growx1", [1, 256], F32, ctx1)
            ngrowx0 = S_("ngrowx0", [1, 256], F32, ctx1)
            ngrowx1 = S_("ngrowx1", [1, 256], F32, ctx1)
            gsum_row1 = S_("gsum_row1", [1, 2], F32, ctx1)
            gcol_p = S_("gcol_p", [128, 2, 2], F32, ctx1)
            gCraw_p = S_("gCraw_p", [128, 2, 2], F32, ctx1)
            grevd_p = S_("grevd_p", [128, 2, 2], F32, ctx1)
            gamc_p = S_("gamc_p", [128, 2, 2], F32, ctx1)
            gCc_p = S_("gCc_p", [128, 2, 2], F32, ctx1)
            grevc_p = S_("grevc_p", [128, 2, 2], F32, ctx1)
            nbetg_p = S_("nbetg_p", [128, 2, 2], F32, ctx1)
            dmA_p = S_("dmA_p", [128, 2, 256], F32, ctx1)
            dmT_p = S_("dmT_p", [128, 2, 256], F32, ctx1)
            a2_p = S_("a2_p", [128, 2, 256], BF16, ctx1)
            at2_p = S_("at2_p", [128, 2, 256], BF16, ctx1)
            p2t2_p = S_("p2t2_p", [128, 2, 256], BF16, ctx1)
            xs2_p = S_("xs2_p", [128, 2, 2, 256], BF16, ctx1)
            ys2_p = S_("ys2_p", [128, 2, 2, 256], BF16, ctx1)
            r2 = S_("r2", [128, 2, 256], BF16, ctx1)
            vb2 = S_("vb2", [128, 256], BF16, ctx1)
            osc2 = S_("osc2", [128, 256], BF16, ctx1)
            st_b = S_("st_b", [128, 2, 128], BF16, ctx1)

            ldvec(lnw_sb, ln1w)
            sG.inc(g.dma_start(out=lnb_sb[:], in_=ln1b.ap().rearrange("(a p) o -> p (a o)", p=128),
                               allow_slow_non_contiguous=True), 16)
            wc_t = wcat.ap().rearrange("(a p) c -> a p c", p=128)
            vq(v.memset(st_b[:, 0], 0.0))
            vq(v.memset(st_b[:, 1], 0.0))
            for a in range(8):
                par = a % 2
                SYW()
                sXC[par].inc(sy.dma_start(out=wt_f[:, par], in_=wc_t[a]), 16)
                VW()
                v.wait_ge(sXC[par].h, sXC[par].v)
                vq(v.tensor_copy(wt_b[:, a], wt_f[:, par]))
                TW()
                t.matmul(pb1[0:1, 0:512], lnb_sb[:, a:a+1], wt_b[:, a, 0:512],
                         start=(a == 0), stop=(a == 7))
                tq(t.matmul(pb2[0:1, 0:260], lnb_sb[:, a:a+1], wt_b[:, a, 512:772],
                            start=(a == 0), stop=(a == 7)))
                VW()
                vq(v.tensor_scalar_mul(wt_b[:, a], wt_f[:, par], lnw_sb[:, a:a+1]))
            AW()
            aq(s.copy(bias_row[:, 0:512], pb1[0:1, 0:512]))
            aq(s.copy(bias_row[:, 512:772], pb2[0:1, 0:260]))

            x_t = x_full.ap().rearrange("(n p) d -> n p d", p=128)

            a1_ev = {}
            for c in range(NCH):
                par = c % 4
                if c == 0:
                    VW()
                if c >= 4:
                    sy.wait_ge(sV.h, a1_ev[c - 4])
                sXC[par].inc(sy.dma_start(out=xc[:, par], in_=x_t[c]), 16)
                v.wait_ge(sXC[par].h, sXC[par].v)
                vq(v.tensor_tensor(sqs[:], xc[:, par], xc[:, par], ALU.mult))
                vqb(v.tensor_reduce(msum[:, c:c+1], xc[:, par], AX.X, ALU.add))
                vqb(v.tensor_reduce(ssum[:, c:c+1], sqs[:], AX.X, ALU.add))
                a1_ev[c] = sV.v
            VW()
            vq(v.tensor_scalar_mul(msum[:], msum[:], 1.0 / D))
            vq(v.tensor_scalar_mul(ssum[:], ssum[:], 1.0 / D))
            vq(v.tensor_tensor(rstd_a[:], msum[:], msum[:], ALU.mult))
            vq(v.tensor_tensor(rstd_a[:], ssum[:], rstd_a[:], ALU.subtract))
            vq(v.tensor_scalar_add(rstd_a[:], rstd_a[:], epsl[:]))
            AW()
            aqb(s.activation(rstd_a[:], rstd_a[:], AF.Ln))
            aqb(s.activation(rstd_a[:], rstd_a[:], AF.Exp, scale=neghalf[:]))

            a2_end = {}
            a2_zev = {}
            for c in range(NCH):
                par = c % 2
                pqk = (pb1, pb2)[par]
                pva = (pb3, pb4)[par]
                tp = 512 * par
                par4 = c % 4
                if c >= 2:
                    gT, gV, gA = a2_end[c - 2]
                    t.wait_ge(sA.h, gA)
                    v.wait_ge(sT.h, gT)
                    s.wait_ge(sV.h, gV)
                if c >= 4:
                    sy.wait_ge(sV.h, a2_zev[c - 4])
                sXC[par4].inc(sy.dma_start(out=xc[:, par4], in_=x_t[c]), 16)
                v.wait_ge(sXC[par4].h, sXC[par4].v)
                sV.inc(v.tensor_scalar(zt[:, par], xc[:, par4], msum[:, c:c+1],
                                       rstd_a[:, c:c+1], ALU.subtract, ALU.mult))
                a2_zev[c] = sV.v
                ev_z = sV.v
                t.wait_ge(sV.h, ev_z)
                for half in range(2):
                    for a in range(4):
                        aa = half * 4 + a
                        ins = t.transpose(pbT[:, tp + a*128:tp + (a+1)*128],
                                          zt[:, par, aa*128:(aa+1)*128], identb[:])
                    sT.inc(ins)
                    ev_tr = sT.v
                    s.wait_ge(sT.h, ev_tr)
                    sA.inc(s.copy(zf[:, par, half*4:half*4+4], pbT[:, tp:tp+512]))
                    if half == 0:
                        t.wait_ge(sA.h, sA.v)
                ev_zf = sA.v
                t.wait_ge(sA.h, ev_zf)
                for a in range(8):
                    t.matmul(pqk[:, 0:512], zf[:, par, a], wt_b[:, a, 0:512],
                             start=(a == 0), stop=False)
                t.matmul(pqk[:, 0:512], ones_row_b[:], bias_row[:, 0:512],
                         start=False, stop=True)
                for a in range(8):
                    t.matmul(pva[:, 0:260], zf[:, par, a], wt_b[:, a, 512:772],
                             start=(a == 0), stop=False)
                sT.inc(t.matmul(pva[:, 0:260], ones_row_b[:], bias_row[:, 512:772],
                                start=False, stop=True))
                ev_pj = sT.v
                s.wait_ge(sT.h, ev_pj)
                aq(s.activation(qt_s[:, c], pqk[:, 0:256], AF.Silu))
                aq(s.activation(kt_s[:, c], pqk[:, 256:512], AF.Silu))
                aq(s.activation(vt_s[:, c], pva[:, 0:256], AF.Silu))
                sA.inc(s.activation(ab_s[:, c], pva[:, 256:260], AF.Tanh, scale=neghalf[:]))
                ev_si = sA.v
                v.wait_ge(sA.h, ev_si)
                vq(v.tensor_scalar(ab_s[:, c], ab_s[:, c], -0.5, 0.5, ALU.mult, ALU.add))
                for i, (buf, lo) in enumerate(((qt_s, 0), (qt_s, 128), (kt_s, 0), (kt_s, 128))):
                    sl = buf[:, c, lo:lo+128]
                    vq(v.tensor_tensor(sqs[:, 0:128], sl, sl, ALU.mult))
                    vqb(v.tensor_reduce(l2s[:, c, i:i+1], sqs[:, 0:128], AX.X, ALU.add))
                a2_end[c] = (sT.v, sV.v, sA.v)

            l2v = l2s[:].rearrange("p a b -> p (a b)")
            AW()
            aqb(s.activation(l2v, l2v, AF.Ln, bias=epsc[:]))
            aqb(s.activation(l2v, l2v, AF.Exp, scale=neghalf[:]))
            VW()
            for c in range(NCH):
                for i, (buf, lo) in enumerate(((qt_s, 0), (qt_s, 128), (kt_s, 0), (kt_s, 128))):
                    sl = buf[:, c, lo:lo+128]
                    vq(v.tensor_scalar_mul(sl, sl, l2s[:, c, i:i+1]))

            endv = {}
            ev_st = None
            for c in range(NCH):
                p = c % 2
                gkq = (pb1, pb2)[p]
                dd = (pb3, pb4)[p]
                xy = (pb5, pb6)[p]
                tp = 512 * p
                grow_r = (growx0, growx1)[p]
                ngrow_r = (ngrowx0, ngrowx1)[p]
                gcol = gcol_p[:, p]
                gCraw = gCraw_p[:, p]
                grevd = grevd_p[:, p]
                gamc = gamc_p[:, p]
                gCc = gCc_p[:, p]
                grevc = grevc_p[:, p]
                nbetg = nbetg_p[:, p]
                dmA = dmA_p[:, p]
                dmT = dmT_p[:, p]
                a2_b = a2_p[:, p]
                at2_b = at2_p[:, p]
                p2t2_b = p2t2_p[:, p]
                qkp = qkf[:, p]
                bA = ab_s[:, c, 0:1]
                bB = ab_s[:, c, 1:2]

                if c >= 2:
                    gT, gV, gA = endv[c - 2]
                    t.wait_ge(sV.h, gV); t.wait_ge(sA.h, gA)
                    v.wait_ge(sT.h, gT); v.wait_ge(sA.h, gA)
                    s.wait_ge(sT.h, gT); s.wait_ge(sV.h, gV)

                sA.inc(s.activation(loga[:], ab_s[:, c, 2:4], AF.Ln))
                ev_a1 = sA.v
                t.wait_ge(sA.h, ev_a1)
                t.matmul(dd[:, 0:2], triuf[:], loga[:], start=True, stop=True)
                t.matmul(dd[0:1, 2:4], ones_col_f[:], loga[:], start=True, stop=True)
                t.matmul(dd[0:1, 4:132], loga[:, 0:1], triuf[:], start=True, stop=True)
                sT.inc(t.matmul(dd[0:1, 132:260], loga[:, 1:2], triuf[:], start=True, stop=True))
                ev_t1 = sT.v
                v.wait_ge(sT.h, ev_t1)
                vq(v.tensor_copy(gcol, dd[:, 0:2]))
                vq(v.tensor_copy(gsum_row1[:], dd[0:1, 2:4]))
                vq(v.tensor_copy(grow_r[:], dd[0:1, 4:260]))
                sV.inc(v.tensor_scalar_mul(ngrow_r[:], dd[0:1, 4:260], -1.0))
                ev_v1 = sV.v
                t.wait_ge(sV.h, ev_v1)
                sT.inc(t.matmul(dd[:, 260:262], ones_row_f[:], gsum_row1[:], start=True, stop=True))
                ev_t2 = sT.v
                v.wait_ge(sT.h, ev_t2)
                vq(v.tensor_copy(gCraw, dd[:, 260:262]))
                sV.inc(v.tensor_tensor(grevd, gCraw, gcol, ALU.subtract))
                ev_v2 = sV.v
                s.wait_ge(sV.h, ev_v2)
                aq(s.activation(gamc, gcol, AF.Exp))
                aq(s.activation(gCc, gCraw, AF.Exp))
                sA.inc(s.activation(grevc, grevd, AF.Exp))
                ev_a2 = sA.v
                v.wait_ge(sA.h, ev_a2)
                vq(v.scalar_tensor_tensor(nbetg, ab_s[:, c, 0:2], -1.0, gamc,
                                          ALU.mult, ALU.mult))
                v.wait_ge(sV.h, sV.v)
                ev_v3 = sV.v

                t.transpose(pbT[:, tp+0:tp+128], qt_s[:, c, 0:128], identb[:])
                t.transpose(pbT[:, tp+128:tp+256], kt_s[:, c, 0:128], identb[:])
                t.transpose(pbT[:, tp+256:tp+384], qt_s[:, c, 128:256], identb[:])
                sT.inc(t.transpose(pbT[:, tp+384:tp+512], kt_s[:, c, 128:256], identb[:]))
                ev_t3 = sT.v
                s.wait_ge(sT.h, ev_t3)
                sA.inc(s.copy(qkp, pbT[:, tp:tp+512]))
                ev_a3 = sA.v
                qfA, kfA = qkp[:, 0:128], qkp[:, 128:256]
                qfB, kfB = qkp[:, 256:384], qkp[:, 384:512]

                t.wait_ge(sA.h, ev_a3)
                t.matmul(gkq[:, 0:128], kfA, kfA, start=True, stop=True)
                t.matmul(gkq[:, 128:256], kfB, kfB, start=True, stop=True)
                t.matmul(gkq[:, 256:384], kfA, qfA, start=True, stop=True)
                t.matmul(gkq[:, 384:512], kfB, qfB, start=True, stop=True)
                for h in range(2):
                    o0 = h * 128
                    t.matmul(dd[:, o0:o0+128], ones_row_f[:], ngrow_r[0:1, o0:o0+128],
                             start=True, stop=False)
                    t.matmul(dd[:, o0:o0+128], grow_r[0:1, o0:o0+128], ones_row_f[:],
                             start=False, stop=True)
                    t.matmul(dd[:, 256+o0:256+o0+128], ones_row_f[:], grow_r[0:1, o0:o0+128],
                             start=True, stop=False)
                    t.matmul(dd[:, 256+o0:256+o0+128], ngrow_r[0:1, o0:o0+128], ones_row_f[:],
                             start=False, stop=True)
                if ev_st is not None:
                    t.wait_ge(sA.h, ev_st)
                t.matmul(pb7[:, 0:128], kfA, st_b[:, 0], start=True, stop=True)
                sT.inc(t.matmul(pb7[:, 128:256], kfB, st_b[:, 1], start=True, stop=True))
                ev_t4 = sT.v

                v.wait_ge(sT.h, ev_t4)
                vq(v.tensor_tensor(dmA, dd[:, 0:256], mnegSs[:], ALU.add))
                sV.inc(v.tensor_tensor(dmT, dd[:, 256:512], mnegUs[:], ALU.add))
                ev_v4 = sV.v
                s.wait_ge(sV.h, ev_v4)
                aq(s.activation(dmA, dmA, AF.Exp))
                sA.inc(s.activation(dmT, dmT, AF.Exp))
                ev_a4 = sA.v
                v.wait_ge(sA.h, ev_a4)
                vq(v.tensor_tensor(dmA, gkq[:, 0:256], dmA, ALU.mult))
                vq(v.tensor_scalar_mul(a2_b[:, 0:128], dmA[:, 0:128], bA))
                vq(v.tensor_scalar_mul(a2_b[:, 128:256], dmA[:, 128:256], bB))
                vq(v.tensor_tensor(p2t2_b, gkq[:, 256:512], dmT, ALU.mult))
                vq(v.tensor_scalar_mul(vb2[:, 0:128], vt_s[:, c, 0:128], bA))
                vq(v.tensor_scalar_mul(vb2[:, 128:256], vt_s[:, c, 128:256], bB))
                vq(v.scalar_tensor_tensor(r2[:, 0, 0:128], pb7[:, 0:128], nbetg[:, 0:1],
                                          vb2[:, 0:128], ALU.mult, ALU.add))
                sV.inc(v.scalar_tensor_tensor(r2[:, 0, 128:256], pb7[:, 128:256], nbetg[:, 1:2],
                                              vb2[:, 128:256], ALU.mult, ALU.add))
                v.wait_ge(sV.h, sV.v)
                ev_v5 = sV.v

                t.wait_ge(sV.h, ev_v5)
                t.transpose(pbT[:, tp+0:tp+128], a2_b[:, 0:128], identb[:])
                sT.inc(t.transpose(pbT[:, tp+128:tp+256], a2_b[:, 128:256], identb[:]))
                ev_t5 = sT.v
                s.wait_ge(sT.h, ev_t5)
                sA.inc(s.copy(at2_b, pbT[:, tp:tp+256]))
                ev_a5 = sA.v

                t.wait_ge(sA.h, ev_a5)
                t.matmul(gkq[:, 0:128], at2_b[:, 0:128], r2[:, 0, 0:128], start=True, stop=True)
                sT.inc(t.matmul(gkq[:, 128:256], at2_b[:, 128:256], r2[:, 0, 128:256],
                                start=True, stop=True))
                ev_t6 = sT.v
                v.wait_ge(sT.h, ev_t6)
                sV.inc(v.tensor_tensor(r2[:, 1], r2[:, 0], gkq[:, 0:256], ALU.subtract))
                ev_v6 = sV.v

                xcur, ycur = a2_b, at2_b
                rp = 1
                ev_a6 = None
                for lev in range(1, 7):
                    if ev_a6 is not None:
                        t.wait_ge(sA.h, ev_a6)
                    if lev < 6:
                        xn = xs2_p[:, p, lev % 2]
                        yn = ys2_p[:, p, lev % 2]
                        t.matmul(xy[:, 0:128], ycur[:, 0:128], xcur[:, 0:128], start=True, stop=True)
                        t.matmul(xy[:, 128:256], ycur[:, 128:256], xcur[:, 128:256], start=True, stop=True)
                        t.matmul(xy[:, 256:384], xcur[:, 0:128], ycur[:, 0:128], start=True, stop=True)
                        sT.inc(t.matmul(xy[:, 384:512], xcur[:, 128:256], ycur[:, 128:256],
                                        start=True, stop=True))
                        ev_t7 = sT.v
                        s.wait_ge(sT.h, ev_t7)
                        aq(s.copy(xn[:], xy[:, 0:256]))
                        sA.inc(s.copy(yn[:], xy[:, 256:512]))
                        ev_a6 = sA.v
                        xcur, ycur = xn, yn
                    else:
                        t.matmul(xy[:, 256:384], xcur[:, 0:128], ycur[:, 0:128], start=True, stop=True)
                        sT.inc(t.matmul(xy[:, 384:512], xcur[:, 128:256], ycur[:, 128:256],
                                        start=True, stop=True))
                        ev_t7 = sT.v
                        s.wait_ge(sT.h, ev_t7)
                        yn = ys2_p[:, p, 0]
                        sA.inc(s.copy(yn[:], xy[:, 256:512]))
                        ev_a6 = sA.v
                        ycur = yn
                    t.wait_ge(sA.h, ev_a6)
                    t.wait_ge(sV.h, ev_v6)
                    t.matmul(gkq[:, 0:128], ycur[:, 0:128], r2[:, rp, 0:128], start=True, stop=True)
                    sT.inc(t.matmul(gkq[:, 128:256], ycur[:, 128:256], r2[:, rp, 128:256],
                                    start=True, stop=True))
                    ev_t8 = sT.v
                    v.wait_ge(sT.h, ev_t8)
                    sV.inc(v.tensor_tensor(r2[:, 1 - rp], r2[:, rp], gkq[:, 0:256], ALU.add))
                    ev_v6 = sV.v
                    rp = 1 - rp

                t.wait_ge(sV.h, ev_v6)
                t.matmul(pb7[:, 0:128], qfA, st_b[:, 0], start=True, stop=True)
                t.matmul(pb7[:, 128:256], qfB, st_b[:, 1], start=True, stop=True)
                t.matmul(pb7[:, 256:384], p2t2_b[:, 0:128], r2[:, rp, 0:128], start=True, stop=True)
                sT.inc(t.matmul(pb7[:, 384:512], p2t2_b[:, 128:256], r2[:, rp, 128:256],
                                start=True, stop=True))
                ev_t9 = sT.v
                s.wait_ge(sT.h, ev_t9)
                aq(s.activation(osc2[:, 0:128], pb7[:, 0:128], AF.Identity, scale=gamc[:, 0:1]))
                sA.inc(s.activation(osc2[:, 128:256], pb7[:, 128:256], AF.Identity, scale=gamc[:, 1:2]))
                ev_a7 = sA.v
                v.wait_ge(sA.h, ev_a7)
                vq(v.tensor_tensor(o_s[:, c, 0:256], osc2[:], pb7[:, 256:512], ALU.add))
                vq(v.tensor_scalar_mul(vb2[:, 0:128], r2[:, rp, 0:128], grevc[:, 0:1]))
                vq(v.tensor_scalar_mul(vb2[:, 128:256], r2[:, rp, 128:256], grevc[:, 1:2]))
                vq(v.tensor_scalar_mul(osc2[:, 0:128], st_b[:, 0], gCc[:, 0:1]))
                sV.inc(v.tensor_scalar_mul(osc2[:, 128:256], st_b[:, 1], gCc[:, 1:2]))
                ev_v8 = sV.v
                t.wait_ge(sV.h, ev_v8)
                t.matmul(pb7[:, 0:256], identb[:], osc2[:], start=True, stop=False)
                t.matmul(pb7[:, 0:128], kt_s[:, c, 0:128], vb2[:, 0:128], start=False, stop=True)
                sT.inc(t.matmul(pb7[:, 128:256], kt_s[:, c, 128:256], vb2[:, 128:256],
                                start=False, stop=True))
                ev_t10 = sT.v
                s.wait_ge(sT.h, ev_t10)
                aq(s.copy(st_b[:, 0], pb7[:, 0:128]))
                sA.inc(s.copy(st_b[:, 1], pb7[:, 128:256]))
                ev_st = sA.v
                endv[c] = (sT.v, sV.v, sA.v)
                if c % 16 == 15:
                    q = c // 16
                    g.wait_ge(sV.h, sV.v)
                    sG.inc(g.dma_start(
                        out=ob_d.ap()[q*2048:(q+1)*2048, :].rearrange("(cc p) u -> p cc u", p=128),
                        in_=o_s[:, q*16:(q+1)*16]), 16)
                    g.wait_ge(sG.h, sG.v)
                    sG.inc(g.collective_compute(
                        "AllGather", ALU.bypass,
                        replica_groups=[[0, 1, 2, 3], [4, 5, 6, 7]],
                        ins=[ob_d.ap()[q*2048:(q+1)*2048, :].opt()],
                        outs=[og_d.ap()[q*8192:(q+1)*8192, :].opt()]), 1)

            g.wait_ge(sG.h, sG.v)

            SYW()
            sy.wait_ge(sG.h, sG.v)
            GW()
            ctx1.close()

            wo_b = S_("wo_b", [128, 8, 1024], BF16)
            stat_m = S_("stat_m", [1, TTW], F32)
            stat_q = S_("stat_q", [1, TTW], F32)
            stat_r = S_("stat_r", [1, TTW], F32)
            sq2 = S_("sq2", [128, TTW], F32)
            sqf = S_("sqf", [128, 4, TTW], F32)
            tsl = S_("tsl", [128, TTW], F32)
            lm = S_("lm", [128, 1], F32)
            mcr = S_("mcr", [128, 8], F32)
            lq = S_("lq", [128, 1], F32)
            lr = S_("lr", [128, 1], F32)
            x2t = S_("x2t", [128, D], F32)
            obuf = S_("obuf", [128, D], F32)
            wk_a = S_("wk_a", [128, 3, 32, 128], BF16)
            wk_b = S_("wk_b", [128, 3, 8, 128], BF16)
            ogt_r = S_("ogt_r", [128, 2, 1024], BF16)
            xstg = S_("xstg", [128, 1, D], F32)
            ogf_t = S_("ogf_t", [128, 8, TTW], BF16)
            x1t = S_("x1t", [128, 8, TTW], F32)
            n2t = S_("n2t", [128, 8, TTW], BF16)
            yt_f = S_("yt_f", [128, 8, TTW], F32)
            ybf = S_("ybf", [128, 8, TTW], BF16)
            hf = S_("hf", [128, 32, TTW], BF16)
            tf = S_("tf", [128, 32, TTW], BF16)

            sG.inc(g.dma_start(out=wo_b[:], in_=wo.ap().rearrange("(hc p) d -> p hc d", p=128)), 16)
            xo_t = x_own.ap().rearrange("(n p) d -> n p d", p=128)

            def ln_stats_rows(srcT):
                VW()
                for half in range(2):
                    for i4 in range(4):
                        dc = half * 4 + i4
                        ins = v.tensor_tensor(sqf[:, i4], srcT[:, dc], srcT[:, dc], ALU.mult)
                    sV.inc(ins)
                    t.wait_ge(sV.h, sV.v)
                    for i4 in range(4):
                        dc = half * 4 + i4
                        t.matmul(pb6[0:1, 0:TTW], ones_col_f[:], srcT[:, dc],
                                 start=(dc == 0), stop=(dc == 7))
                    for i4 in range(4):
                        dc = half * 4 + i4
                        ins2 = t.matmul(pb7[0:1, 0:TTW], ones_col_f[:], sqf[:, i4],
                                        start=(dc == 0), stop=(dc == 7))
                    tq(ins2)
                    if half == 0:
                        v.wait_ge(sT.h, sT.v)
                VW()
                vq(v.tensor_scalar_mul(stat_m[0:1, :], pb6[0:1, 0:TTW], 1.0 / D))
                vq(v.tensor_scalar_mul(stat_q[0:1, :], pb7[0:1, 0:TTW], 1.0 / D))
                vq(v.tensor_tensor(stat_r[0:1, :], stat_m[0:1, :], stat_m[0:1, :], ALU.mult))
                vq(v.tensor_tensor(stat_r[0:1, :], stat_q[0:1, :], stat_r[0:1, :], ALU.subtract))
                vq(v.tensor_scalar_add(stat_r[0:1, :], stat_r[0:1, :], eps1p[:]))
                AW()
                aqb(s.activation(stat_r[0:1, :], stat_r[0:1, :], AF.Ln))
                aqb(s.activation(stat_r[0:1, :], stat_r[0:1, :], AF.Exp, scale=nh1p[:]))
                TW()
                t.matmul(pb6[:, 0:TTW], ones_row_f[:], stat_m[0:1, :], start=True, stop=True)
                tq(t.matmul(pb7[:, 0:TTW], ones_row_f[:], stat_r[0:1, :], start=True, stop=True))

            def stream_w(slot, src_d, colblk, ntiles, ring_t):
                ap = src_d.ap().rearrange("(a p) o -> p a o", p=128)[:, :, colblk*128:(colblk+1)*128]
                sem = (sWA if ring_t is wk_a else sWB)[slot]
                sem.inc(g.dma_start(out=ring_t[:, slot, 0:ntiles], in_=ap), 16)
                return (sem.h, sem.v)

            def mm_stage(wd, ncb, nk, rhs_at, evac, wk=None, lhsT_at=None):
                banks = (pb3, pb4)
                gv = {}
                mm_ev = {}
                if wd is not None:
                    for pre in range(min(3, ncb)):
                        g.wait_ge(sT.h, sT.v)
                        gv[pre] = stream_w(pre % 3, wd, pre, nk, wk)
                bank_gate = [None, None]
                for cb in range(ncb):
                    bank = banks[cb % 2]
                    if wd is not None:
                        t.wait_ge(*gv[cb])
                    if bank_gate[cb % 2] is not None:
                        sem, val = bank_gate[cb % 2]
                        t.wait_ge(sem, val)
                    for k in range(nk):
                        lh = wk[:, cb % 3, k] if wd is not None else lhsT_at(cb, k)
                        ins = t.matmul(bank[:, 0:TTW], lh, rhs_at(k),
                                       start=(k == 0), stop=(k == nk - 1))
                    tq(ins)
                    mmv = sT.v
                    mm_ev[cb] = mmv
                    if wd is not None and cb + 3 < ncb:
                        g.wait_ge(sT.h, mm_ev[cb])
                        gv[cb + 3] = stream_w(cb % 3, wd, cb + 3, nk, wk)
                    bank_gate[cb % 2] = evac(cb, bank, mmv)

            for tt in range(NTT):
                for ci in range(TTW // 128):
                    cc = tt * (TTW // 128) + ci
                    par = 0
                    GW()
                    for i in range(4):
                        n = cc * 4 + i
                        sOG[par].inc(g.indirect_dma_start(
                            ogt_r[:, par, i*256:(i+1)*256], None, og_d.ap(),
                            bass.IndirectOffsetOnAxis(ap=gidx_s[:, n:n+1], axis=0)), 16)
                    SYW()
                    sXC[par].inc(sy.dma_start(out=xstg[:, par], in_=xo_t[cc]), 16)
                    TW()
                    t.wait_ge(sOG[par].h, sOG[par].v)
                    for hc in range(8):
                        ins = t.transpose(pbT[:, hc*128:(hc+1)*128], ogt_r[:, par, hc*128:(hc+1)*128],
                                          identb[:])
                    tq(ins)
                    AW()
                    for hc in range(8):
                        ins = s.copy(ogf_t[:, hc, ci*128:(ci+1)*128], pbT[:, hc*128:(hc+1)*128])
                    aq(ins)
                    TW()
                    t.wait_ge(sXC[par].h, sXC[par].v)
                    for a in range(8):
                        pbank = pb1 if a < 4 else pb2
                        ins = t.transpose(pbank[:, (a % 4)*128:(a % 4 + 1)*128],
                                          xstg[:, par, a*128:(a+1)*128], identf[:])
                    tq(ins)
                    AW()
                    for a in range(8):
                        pbank = pb1 if a < 4 else pb2
                        ins = s.copy(x1t[:, a, ci*128:(ci+1)*128], pbank[:, (a % 4)*128:(a % 4 + 1)*128])
                    aq(ins)

                GW()
                sG.inc(g.dma_start(out=xof_d.ap()[:, :, tt*TTW:(tt+1)*TTW].rearrange("a p n -> p a n"),
                                   in_=x1t[:]), 16)
                g.wait_ge(sG.h, sG.v)
                TW()
                def wo_evac(cb, bank, mmv):
                    v.wait_ge(sT.h, mmv)
                    vqb(v.scalar_tensor_tensor(x1t[:, cb], bank[:, 0:TTW], 1.0,
                                               x1t[:, cb], ALU.mult, ALU.add))
                    return (sV.h, sV.v)
                mm_stage(None, 8, 8, lambda k: ogf_t[:, k], wo_evac,
                         lhsT_at=lambda cb, k: wo_b[:, k, cb*128:(cb+1)*128])

                ln_stats_rows(x1t)
                for dc in range(8):
                    VW()
                    vq(v.tensor_tensor(sq2[:], x1t[:, dc], pb6[:, 0:TTW], ALU.subtract))
                    vq(v.tensor_tensor(sq2[:], sq2[:], pb7[:, 0:TTW], ALU.mult))
                    vq(v.tensor_scalar(n2t[:, dc], sq2[:], ln2w_sb[:, dc:dc+1],
                                       ln2b_sb[:, dc:dc+1], ALU.mult, ALU.add))

                TW()
                def w1_evac(cb, bank, mmv):
                    s.wait_ge(sT.h, mmv)
                    aq(s.activation(hf[:, cb], bank[:, 0:TTW], AF.Gelu, bias=bvec1[:, cb:cb+1]))
                    return (sA.h, sA.v)
                mm_stage(w1, 32, 8, lambda k: n2t[:, k], w1_evac, wk_a)
                TW()
                def w2_evac(cb, bank, mmv):
                    v.wait_ge(sT.h, mmv)
                    vqb(v.scalar_tensor_tensor(x1t[:, cb], bank[:, 0:TTW], bvec2[:, cb:cb+1],
                                               x1t[:, cb], ALU.add, ALU.add))
                    return (sV.h, sV.v)
                mm_stage(w2, 8, 32, lambda k: hf[:, k], w2_evac, wk_a)

                GW()
                sG.inc(g.dma_start(out=x1t[:], in_=xof_d.ap()[:, :, tt*TTW:(tt+1)*TTW].rearrange("a p n -> p a n"),
                                   accum_op=ALU.add), 16)
                g.wait_ge(sG.h, sG.v)
                SYW()
                sy.wait_ge(sG.h, sG.v)
                v.wait_ge(sG.h, sG.v)
                t.wait_ge(sG.h, sG.v)
                s.wait_ge(sG.h, sG.v)
                ln_stats_rows(x1t)
                for dc in range(8):
                    VW()
                    vq(v.tensor_tensor(yt_f[:, dc], x1t[:, dc], pb6[:, 0:TTW], ALU.subtract))
                    vq(v.tensor_tensor(yt_f[:, dc], yt_f[:, dc], pb7[:, 0:TTW], ALU.mult))
                    vq(v.tensor_copy(ybf[:, dc], yt_f[:, dc]))

                TW()
                w12_banks = ((pb3, pb4), (pb5, pb6))
                w12_gv = {}
                for pre in range(3):
                    g.wait_ge(sT.h, sT.v)
                    ga = stream_w(pre % 3, fw1, pre, 8, wk_a)
                    gb = stream_w(pre % 3, fw2, pre, 8, wk_b)
                    w12_gv[pre] = (ga, gb)
                w12_gate = [None, None]
                tsl2 = (tsl, sq2)
                for cb in range(32):
                    bA, bB = w12_banks[cb % 2]
                    t.wait_ge(*w12_gv[cb][0])
                    t.wait_ge(*w12_gv[cb][1])
                    if w12_gate[cb % 2] is not None:
                        sem, val = w12_gate[cb % 2]
                        t.wait_ge(sem, val)
                    for dc in range(8):
                        t.matmul(bA[:, 0:TTW], wk_a[:, cb % 3, dc], ybf[:, dc],
                                 start=(dc == 0), stop=(dc == 7))
                    for dc in range(8):
                        ins = t.matmul(bB[:, 0:TTW], wk_b[:, cb % 3, dc], ybf[:, dc],
                                       start=(dc == 0), stop=(dc == 7))
                    tq(ins)
                    mmv = sT.v
                    if cb + 3 < 32:
                        g.wait_ge(sT.h, mmv)
                        ga = stream_w(cb % 3, fw1, cb + 3, 8, wk_a)
                        gb = stream_w(cb % 3, fw2, cb + 3, 8, wk_b)
                        w12_gv[cb + 3] = (ga, gb)
                    s.wait_ge(sT.h, mmv)
                    aq(s.activation(tsl2[cb % 2][:, 0:TTW], bA[:, 0:TTW], AF.Silu,
                                    scale=svec1[:, cb:cb+1]))
                    v.wait_ge(sA.h, sA.v)
                    vqb(v.scalar_tensor_tensor(tf[:, cb], bB[:, 0:TTW], svec2[:, cb:cb+1],
                                               tsl2[cb % 2][:, 0:TTW], ALU.mult, ALU.mult))
                    w12_gate[cb % 2] = (sV.h, sV.v)
                TW()
                def h1_evac(cb, bank, mmv):
                    s.wait_ge(sT.h, mmv)
                    aq(s.activation(hf[:, cb], bank[:, 0:TTW], AF.Gelu, scale=svh1[:, cb:cb+1]))
                    return (sA.h, sA.v)
                mm_stage(fh1, 32, 32, lambda k: tf[:, k], h1_evac, wk_a)
                TW()
                def h2_evac(cb, bank, mmv):
                    s.wait_ge(sT.h, mmv)
                    aq(s.activation(tf[:, cb], bank[:, 0:TTW], AF.Gelu, scale=svh2[:, cb:cb+1]))
                    return (sA.h, sA.v)
                mm_stage(fh2, 32, 32, lambda k: hf[:, k], h2_evac, wk_a)
                TW()
                def w3_evac(cb, bank, mmv):
                    s.wait_ge(sT.h, mmv)
                    aq(s.activation(x1t[:, cb], bank[:, 0:TTW], AF.Identity, scale=sv3[:, cb:cb+1]))
                    v.wait_ge(sA.h, sA.v)
                    vq(v.tensor_tensor(x1t[:, cb], x1t[:, cb], yt_f[:, cb], ALU.add))
                    return (sV.h, sV.v)
                mm_stage(fw3, 8, 32, lambda k: tf[:, k], w3_evac, wk_a)

                ln_stats_rows(x1t)
                TW()
                for ci in range(4):
                    t.transpose(pb5[:, ci:ci+1], stat_m[0:1, ci*128:(ci+1)*128], identf[0:1, 0:1])
                for ci in range(4):
                    ins = t.transpose(pb5[:, 4+ci:5+ci], stat_r[0:1, ci*128:(ci+1)*128], identf[0:1, 0:1])
                tq(ins)
                v.wait_ge(sT.h, sT.v)
                sV.inc(v.tensor_copy(mcr[:], pb5[:, 0:8]))
                v.wait_ge(sV.h, sV.v)
                ev_mcr = sV.v
                dma_ev = {}
                for ci in range(TTW // 128):
                    cc = tt * (TTW // 128) + ci
                    ob = (obuf, x2t)[ci % 2]
                    t.wait_ge(sV.h, sV.v)
                    for a in range(8):
                        pbank = pb1 if a < 4 else pb2
                        ins = t.transpose(pbank[:, (a % 4)*128:(a % 4 + 1)*128],
                                          x1t[:, a, ci*128:(ci+1)*128], identf[:])
                    tq(ins)
                    v.wait_ge(sT.h, sT.v)
                    if ci >= 2:
                        v.wait_ge(sO.h, dma_ev[ci - 2])
                    for a in range(8):
                        pbank = pb1 if a < 4 else pb2
                        ins = v.tensor_scalar(ob[:, a*128:(a+1)*128],
                                              pbank[:, (a % 4)*128:(a % 4 + 1)*128],
                                              mcr[:, ci:ci+1], mcr[:, 4+ci:5+ci],
                                              ALU.subtract, ALU.mult)
                        sV.inc(ins)
                    v.wait_ge(sV.h, sV.v)
                    sy.wait_ge(sV.h, sV.v)
                    sO.inc(sy.dma_start(out=out_d.ap().rearrange("(n p) d -> n p d", p=128)[cc],
                                        in_=ob[:]), 16)
                    dma_ev[ci] = sO.v

            sy.wait_ge(sO.h, sO.v)
            g.wait_ge(sO.h, sO.v)

        block.tensor(emit)
    ctx.close()
    return nc


_NC_CACHE = []


def kernel(**inputs):
    x = np.ascontiguousarray(np.asarray(inputs["L0_attn_x"], dtype=np.float32))
    Wq = np.asarray(inputs["L0_attn_Wq"], dtype=np.float32)
    Wk = np.asarray(inputs["L0_attn_Wk"], dtype=np.float32)
    Wv = np.asarray(inputs["L0_attn_Wv"], dtype=np.float32)
    Wb = np.asarray(inputs["L0_attn_Wb"], dtype=np.float32)
    Wa = np.asarray(inputs["L0_attn_Wa"], dtype=np.float32)
    Wo = np.ascontiguousarray(np.asarray(inputs["L0_attn_Wo"], dtype=np.float32))
    ln1w_np = np.ascontiguousarray(np.asarray(inputs["L0_attn_ln1_w"], np.float32).reshape(D, 1))
    ln1b_np = np.ascontiguousarray(np.asarray(inputs["L0_attn_ln1_b"], np.float32).reshape(D, 1))
    ln2w_np = np.ascontiguousarray(np.asarray(inputs["L0_attn_ln2_w"], np.float32).reshape(D, 1))
    ln2b_np = np.ascontiguousarray(np.asarray(inputs["L0_attn_ln2_b"], np.float32).reshape(D, 1))
    W1 = np.ascontiguousarray(np.asarray(inputs["L0_attn_W1"], np.float32))
    b1_np = np.ascontiguousarray(np.asarray(inputs["L0_attn_b1"], np.float32).reshape(FF, 1))
    W2 = np.ascontiguousarray(np.asarray(inputs["L0_attn_W2"], np.float32))
    b2_np = np.ascontiguousarray(np.asarray(inputs["L0_attn_b2"], np.float32).reshape(D, 1))
    fw1_np = np.ascontiguousarray(np.asarray(inputs["L0_ffn_w1_w"], np.float32).T)
    fs1_np = np.ascontiguousarray(np.asarray(inputs["L0_ffn_w1_s"], np.float32).reshape(FF, 1))
    fw2_np = np.ascontiguousarray(np.asarray(inputs["L0_ffn_w2_w"], np.float32).T)
    fs2_np = np.ascontiguousarray(np.asarray(inputs["L0_ffn_w2_s"], np.float32).reshape(FF, 1))
    fh1_np = np.ascontiguousarray(np.asarray(inputs["L0_ffn_h1_w"], np.float32).T)
    fsh1_np = np.ascontiguousarray(np.asarray(inputs["L0_ffn_h1_s"], np.float32).reshape(FF, 1))
    fh2_np = np.ascontiguousarray(np.asarray(inputs["L0_ffn_h2_w"], np.float32).T)
    fsh2_np = np.ascontiguousarray(np.asarray(inputs["L0_ffn_h2_s"], np.float32).reshape(FF, 1))
    fw3_np = np.ascontiguousarray(np.asarray(inputs["L0_ffn_w3_w"], np.float32).T)
    fs3_np = np.ascontiguousarray(np.asarray(inputs["L0_ffn_w3_s"], np.float32).reshape(D, 1))

    identb_np = np.eye(128, dtype=bfd)
    identf_np = np.eye(128, dtype=np.float32)
    triuf_np = np.triu(np.ones((128, 128), np.float32))
    mnegS_np = (-1e30 * (1 - np.tril(np.ones((128, 128), np.float32), -1))).astype(np.float32)
    mnegU_np = (-1e30 * (1 - triuf_np)).astype(np.float32)

    in_maps = []
    for c in range(NCORE):
        b, j = c // 4, c % 4
        gidx = np.empty((128, 64), np.uint32)
        for n in range(64):
            cc, i = n // 4, n % 4
            gidx[:, n] = j * 8192 + i * 2048 + cc * 128 + np.arange(128, dtype=np.uint32)
        in_maps.append(dict(
            x_full=np.ascontiguousarray(x[b]),
            x_own=np.ascontiguousarray(x[b, j*TOK:(j+1)*TOK]),
            wcat=np.ascontiguousarray(np.concatenate(
                [Wq[:, 256*j:256*j+256], Wk[:, 256*j:256*j+256], Wv[:, 256*j:256*j+256],
                 Wb[:, 2*j:2*j+2], Wa[:, 2*j:2*j+2]], axis=1)),
            ln1w=ln1w_np, ln1b=ln1b_np, wo=Wo,
            w1=W1, b1c=b1_np, w2=W2, b2c=b2_np, ln2w=ln2w_np, ln2b=ln2b_np,
            fw1=fw1_np, fs1=fs1_np, fw2=fw2_np, fs2=fs2_np,
            fh1=fh1_np, fsh1=fsh1_np, fh2=fh2_np, fsh2=fsh2_np,
            fw3=fw3_np, fs3=fs3_np,
            identb_i=identb_np, identf_i=identf_np, triuf_i=triuf_np,
            mnegS_i=np.concatenate([mnegS_np, mnegS_np], 1),
            mnegU_i=np.concatenate([mnegU_np, mnegU_np], 1), gidx_i=gidx,
        ))

    if not _NC_CACHE:
        _NC_CACHE.append(build_program())
    nc = _NC_CACHE[0]
    res = run_bass_kernel_spmd(nc, in_maps, core_ids=list(range(NCORE)))
    out = np.empty((B, T, D), np.float32)
    for c in range(NCORE):
        b, j = c // 4, c % 4
        out[b, j*TOK:(j+1)*TOK] = res.results[c]["out"]
    return out
